# version 38
# baseline (speedup 1.0000x reference)
import contextlib
import numpy as np
import concourse.bass as bass
import concourse.mybir as mybir

F32 = mybir.dt.float32
BF16 = mybir.dt.bfloat16
AX = mybir.AxisListType.X
ALU = mybir.AluOpType
ACTF = mybir.ActivationFunctionType

ENGS = ("sync", "tensor", "vector", "scalar", "gpsimd")


def _ap_region(ap):
    t = ap.tensor
    name = t.name
    shape = list(t.shape)
    space = getattr(t, "space", None)
    is_dram = not hasattr(t, "base_partition")
    if is_dram:
        off = ap.offset
        span = 1
        for step, count in ap.ap:
            if count > 1:
                span += (count - 1) * step
        return (name, "dram", 0, 1, off, off + span)
    rowelems = 1
    for s in shape[1:]:
        rowelems *= s
    off = ap.offset
    plo = off // rowelems
    flo = off % rowelems
    pspan = 1
    fspan = 1
    for step, count in ap.ap:
        if count <= 1:
            continue
        if step >= rowelems and rowelems > 0:
            pspan += (count - 1) * (step // rowelems)
        else:
            fspan += (count - 1) * step
    base = getattr(t, "base_partition", 0) or 0
    if type(t).__name__.startswith("PSum"):
        return (name, "psum", 0, 128, 0, rowelems)
    return (name, "sbuf", base + plo, base + plo + pspan, flo, flo + fspan)


def _overlap(a, b):
    return (a[0] == b[0] and a[2] < b[3] and b[2] < a[3] and a[4] < b[5] and b[4] < a[5])


class Op:
    __slots__ = ("eng", "emit", "kind", "regions_r", "regions_w", "deps",
                 "needs_inc", "inc_val", "dma_sem", "dma_val", "issue_wait",
                 "seq", "emitted_idx")

    def __init__(self, eng, emit, kind):
        self.eng = eng
        self.emit = emit
        self.kind = kind
        self.regions_r = []
        self.regions_w = []
        self.deps = []
        self.needs_inc = False
        self.inc_val = None
        self.dma_sem = None
        self.dma_val = None
        self.issue_wait = None
        self.seq = 0


class Prog:
    def __init__(self, nc, ctx):
        self.nc = nc
        self.ops = []
        self.writers = {}
        self.readers = {}
        self.eng_sem = {e: ctx.enter_context(nc.semaphore(f"es_{e}")) for e in ENGS}
        self.eng_cnt = {e: 0 for e in ENGS}
        self.dma_pools = {}
        for q, n in (("ld", 24), ("g", 8), ("cc", 2), ("fin", 4), ("s", 16)):
            self.dma_pools[q] = {
                "sems": [[ctx.enter_context(nc.semaphore(f"q_{q}{i}")), 0] for i in range(n)],
                "rot": 0}
        self.hw = {}
        self.warm_ap = None
        self.warm_ops = None
        self.phase_idx = 0
        self.phase_bounds = []

    def _add(self, eng, emit, reads, writes, kind="compute", queue="ld", inck=16):
        op = Op(eng, emit, kind)
        deps = []
        for ap in reads:
            r = _ap_region(ap)
            op.regions_r.append(r)
            for (wr, wop) in self.writers.get(r[0], ()):
                if _overlap(r, wr):
                    deps.append(wop)
        for ap in writes:
            r = _ap_region(ap)
            op.regions_w.append(r)
            for (wr, wop) in self.writers.get(r[0], ()):
                if _overlap(r, wr):
                    deps.append(wop)
            for (rr, rop) in self.readers.get(r[0], ()):
                if _overlap(r, rr):
                    deps.append(rop)
        op.deps = deps
        for d in deps:
            if d.kind == "compute" and d.eng != eng:
                d.needs_inc = True
        if kind in ("dma", "cc"):
            pool = self.dma_pools[queue]
            slot = pool["sems"][pool["rot"]]
            pool["rot"] = (pool["rot"] + 1) % len(pool["sems"])
            if slot[1] > 0:
                op.issue_wait = (slot[0], slot[1])
            slot[1] += inck
            op.dma_sem = slot[0]
            op.dma_val = slot[1]
        for r in op.regions_w:
            lst = self.writers.setdefault(r[0], [])
            lst[:] = [(wr, wop) for (wr, wop) in lst
                      if not (wr[2] >= r[2] and wr[3] <= r[3] and wr[4] >= r[4] and wr[5] <= r[5])]
            lst.append((r, op))
            rl = self.readers.get(r[0])
            if rl:
                rl[:] = [(rr, rop) for (rr, rop) in rl if not _overlap(r, rr)]
        for r in op.regions_r:
            self.readers.setdefault(r[0], []).append((r, op))
        self.ops.append(op)
        return op

    def phase(self, name=None):
        self.phase_idx += 1
        for op in self.ops:
            if op.kind == "compute" and op.needs_inc:
                self.eng_cnt[op.eng] += 1
                op.inc_val = self.eng_cnt[op.eng]
            op.seq = 1
        per_eng = {e: [] for e in ENGS}
        for op in self.ops:
            per_eng[op.eng].append(op)
        prog = self

        def make_body(eng, ops):
            def body(e):
                drained_upto = [0]
                emitted = [0]
                for op in ops:
                    need_drain = False
                    if op.issue_wait is not None:
                        h, v = op.issue_wait
                        key = (eng, id(h))
                        if prog.hw.get(key, -1) < v:
                            e.wait_ge(h, v)
                            prog.hw[key] = v
                    for d in op.deps:
                        if d.kind in ("dma", "cc"):
                            key = (eng, id(d.dma_sem))
                            if prog.hw.get(key, -1) < d.dma_val:
                                e.wait_ge(d.dma_sem, d.dma_val)
                                prog.hw[key] = d.dma_val
                        elif d.eng == eng:
                            if eng in ("vector", "scalar", "gpsimd") and d.emitted_idx >= drained_upto[0]:
                                need_drain = True
                        else:
                            key = (eng, id(prog.eng_sem[d.eng]))
                            if d.inc_val is not None and prog.hw.get(key, -1) < d.inc_val:
                                e.wait_ge(prog.eng_sem[d.eng], d.inc_val)
                                prog.hw[key] = d.inc_val
                    if need_drain:
                        e.drain()
                        drained_upto[0] = emitted[0]
                    inc = None
                    if op.kind == "compute" and op.inc_val is not None:
                        inc = (prog.eng_sem[eng], 1)
                    elif op.kind == "dma":
                        inc = (op.dma_sem, 16)
                    elif op.kind == "cc":
                        inc = (op.dma_sem, 1)
                    if not hasattr(prog, "imap"):
                        prog.imap = []
                    prog.imap.append(
                        (prog.nc.n_instructions(), eng, op.kind,
                         [r[0] for r in op.regions_w], [r[0] for r in op.regions_r]))
                    ret = op.emit(e)
                    if inc is not None and ret is not None:
                        ret.then_inc(*inc)
                    op.emitted_idx = emitted[0]
                    emitted[0] += 1
            return body

        idx = {e: 0 for e in ENGS}
        for op in self.ops:
            op.emitted_idx = idx[op.eng]
            idx[op.eng] += 1

        import contextlib as _ctxlib
        _i0 = self.nc.n_instructions()
        scope = self.nc.named_scope(name) if name else _ctxlib.nullcontext()
        with scope:
            with self.nc.Block(name) as block:
                for eng in ENGS:
                    if per_eng[eng]:
                        getattr(block, eng)(make_body(eng, per_eng[eng]))
        self.phase_bounds.append((name, _i0, self.nc.n_instructions()))
        for d in (self.writers, self.readers):
            for k in list(d.keys()):
                d[k] = [(r, op) for (r, op) in d[k] if op.kind in ("dma", "cc")]
        self.ops = []

    def ld(self, out, in_, queue="ld"):
        self._add("sync", lambda e, o=out, i=in_: e.dma_start(out=o, in_=i),
                  [in_], [out], kind="dma", queue=queue)

    def st(self, out, in_, queue="ld"):
        self._add("sync", lambda e, o=out, i=in_: e.dma_start(out=o, in_=i),
                  [in_], [out], kind="dma", queue=queue)

    def gld(self, out, in_, queue="g"):
        self._add("gpsimd", lambda e, o=out, i=in_: e.dma_start(out=o, in_=i),
                  [in_], [out], kind="dma", queue=queue)

    def sld(self, out, in_, queue="s"):
        self._add("scalar", lambda e, o=out, i=in_: e.dma_start(out=o, in_=i),
                  [in_], [out], kind="dma", queue=queue)

    def cc(self, kind, alu, groups, in_ap, out_ap):
        self._add("gpsimd", lambda e: e.collective_compute(
            kind, alu, replica_groups=groups, ins=[in_ap], outs=[out_ap]),
            [in_ap], [out_ap], kind="cc", queue="cc", inck=1)

    def mm(self, out, lhsT, rhs, start=True, stop=True):
        self._add("tensor", lambda e, o=out, l=lhsT, r=rhs, s1=start, s2=stop:
                  e.matmul(o, l, r, start=s1, stop=s2), [lhsT, rhs], [out])

    def mmg(self, out, pairs):
        n = len(pairs)
        def emit(e, o=out, pairs=tuple(pairs), n=n):
            ret = None
            for i, (l, r) in enumerate(pairs):
                ret = e.matmul(o, l, r, start=(i == 0), stop=(i == n - 1))
            return ret
        reads = [out]
        for l, r in pairs:
            reads += [l, r]
        self._add("tensor", emit, reads, [out])

    def tr(self, out, in_, ident):
        self._add("tensor", lambda e, o=out, i=in_, d=ident: e.transpose(o, i, d),
                  [in_, ident], [out])

    def cp(self, eng, out, in_):
        self._add(eng, lambda e, o=out, i=in_: e.tensor_copy(out=o, in_=i), [in_], [out])

    def ts(self, eng, out, in0, s1, op0, s2=None, op1=None):
        reads = [in0] + [s for s in (s1, s2) if isinstance(s, bass.AP)]
        self._add(eng, lambda e, o=out, i=in0, s1=s1, s2=s2, op0=op0, op1=op1:
                  e.tensor_scalar(out=o, in0=i, scalar1=s1, scalar2=s2, op0=op0, **({"op1": op1} if op1 is not None else {})),
                  reads, [out])

    def tt(self, eng, out, in0, in1, op):
        self._add(eng, lambda e, o=out, a=in0, b=in1, op=op: e.tensor_tensor(out=o, in0=a, in1=b, op=op),
                  [in0, in1], [out])

    def stt(self, eng, out, in0, scalar, in1, op0, op1):
        reads = [in0, in1] + ([scalar] if isinstance(scalar, bass.AP) else [])
        self._add(eng, lambda e, o=out, a=in0, s=scalar, b=in1, op0=op0, op1=op1:
                  e.scalar_tensor_tensor(out=o, in0=a, scalar=s, in1=b, op0=op0, op1=op1),
                  reads, [out])

    def red(self, eng, out, in_, op, absval=False):
        self._add(eng, lambda e, o=out, i=in_, op=op, a=absval:
                  e.tensor_reduce(out=o, in_=i, axis=AX, op=op, apply_absolute_value=a),
                  [in_], [out])

    def act(self, out, in_, func, bias=0.0, scale=1.0, accum=None):
        reads = [in_] + [s for s in (bias, scale) if isinstance(s, bass.AP)]
        writes = [out] + ([accum] if accum is not None else [])
        if isinstance(bias, float):
            bias = np.float32(bias)
        if isinstance(scale, float):
            scale = np.float32(scale)
        self._add("scalar", lambda e, o=out, i=in_, f=func, b=bias, s=scale, a=accum:
                  e.activation(out=o, in_=i, func=f, bias=b, scale=s, accum_out=a),
                  reads, writes)

    def recip(self, out, in_):
        self._add("vector", lambda e, o=out, i=in_: e.reciprocal(out=o, in_=i), [in_], [out])

    def mset(self, eng, ap, v):
        self._add(eng, lambda e, a=ap, v=v: e.memset(a, v), [], [ap])

    def iota(self, ap, pattern, mult):
        self._add("gpsimd", lambda e, a=ap, p=pattern, m=mult: e.iota(a, p, channel_multiplier=m,
                  allow_small_or_imprecise_dtypes=True), [], [ap])

    def raw(self, eng, emit, reads, writes):
        self._add(eng, emit, reads, writes)

    def warm(self):
        if self.warm_ap is None:
            return
        o, l, r = self.warm_ap, self.warm_ops[0], self.warm_ops[1]
        self._add("tensor", lambda e, o=o, l=l, r=r: e.matmul(o, l, r, start=True, stop=True),
                  [l, r], [o])

    def dma_fence(self):
        self.wait_all_dma("sync")

    def wait_all_dma(self, eng):
        for q, pool in self.dma_pools.items():
            for (h, v) in pool["sems"]:
                if v > 0:
                    self._add(eng, lambda e, h=h, v=v: e.wait_ge(h, v), [], [])


import contextlib


M1 = float(np.float32(1.5 * 2.0**23))
M2 = float(np.float32(1.5 * 2.0**52))
DM = float(1.5 * 2.0**52 - 1.5 * 2.0**23)


class CFG:
    def __init__(self, Tc=2048, n_cores=8, group=4, sup=4):
        self.E = 1024
        self.H = 16
        self.Dh = 64
        self.INNER = 4096
        self.CH = 128
        self.Tc = Tc
        self.n_st = Tc // 128
        self.sup = sup
        self.n_sup = self.n_st // sup
        self.n_cores = n_cores
        self.group = group
        assert self.n_st % sup == 0 and n_cores % group == 0


def _build_slopes(n):
    def po2(m):
        start = 2.0 ** (-2.0 ** (-(np.log2(m) - 3)))
        return [start * start**i for i in range(m)]
    if np.log2(n).is_integer():
        return po2(n)
    closest = 2 ** int(np.floor(np.log2(n)))
    return po2(closest) + _build_slopes(2 * closest)[0::2][: n - closest]


def host_tables(cfg):
    H, CH, n_st = cfg.H, cfg.CH, cfg.n_st
    s32 = np.array(_build_slopes(H), dtype=np.float32)
    s32 = s32 * (1.0 - 0.0 / 11.0 + 1e-05)
    s = s32.astype(np.float64)
    idx = np.arange(CH, dtype=np.float64)
    diff = idx[:, None] - idx[None, :]
    mask = np.where(diff >= 0, np.exp(-s[:, None, None] * diff), 0.0)
    maskT = np.ascontiguousarray(mask.transpose(0, 2, 1)).astype(np.float32)
    lam_q = np.exp(-s[:, None] * (idx + 1.0))
    lam_k = np.exp(-s[:, None] * (CH - 1.0 - idx))
    lam_c = np.exp(-s * CH)
    gam = lam_q[:, :, None] * (lam_c[:, None, None] ** np.arange(n_st)[None, None, :])
    Dh = 64
    lamk_full = np.repeat(lam_k[:, None, :], Dh, axis=1)
    lamk_full = lamk_full.transpose(2, 0, 1).reshape(CH, H * Dh)
    lamc_full = np.broadcast_to(np.repeat(lam_c, Dh).reshape(1, H * Dh), (Dh, H * Dh))
    gf = gam.transpose(2, 1, 0)
    gam_full = np.repeat(gf[:, :, :, None], Dh, axis=3).reshape(n_st * CH, H * Dh)
    return dict(
        maskT=maskT,
        lamq=np.ascontiguousarray(lam_q.T).astype(np.float32),
        lamc=[float(v) for v in lam_c],
        lamk_full=np.ascontiguousarray(lamk_full).astype(np.float32),
        lamc_full=np.ascontiguousarray(lamc_full).astype(np.float32),
        gam_full=np.ascontiguousarray(gam_full).astype(np.float32),
    )


def host_cvec(cfg, pos):
    H, n_st = cfg.H, cfg.n_st
    s32 = np.array(_build_slopes(H), dtype=np.float32)
    s32 = s32 * (1.0 - 0.0 / 11.0 + 1e-05)
    lam_c = np.exp(-s32.astype(np.float64) * cfg.CH)
    cv = np.zeros((cfg.group, H), dtype=np.float64)
    for j in range(pos):
        cv[j] = lam_c ** (n_st * (pos - 1 - j))
    out = np.broadcast_to(cv.reshape(1, cfg.group * H), (64, cfg.group * H))
    return np.ascontiguousarray(out).astype(np.float32)


def build_nc(cfg, dbg=False):
    nc = bass.Bass("TRN2", target_bir_lowering=False)
    E, H, Dh, INNER, CH = cfg.E, cfg.H, cfg.Dh, cfg.INNER, cfg.CH
    Tc, n_st, SUP, n_sup = cfg.Tc, cfg.n_st, cfg.sup, cfg.n_sup
    NOC = E // 128
    NO5 = INNER // 512
    NOCI = INNER // 128
    SC = min(512, Tc)
    NS5 = Tc // SC
    lamc = host_tables(cfg)["lamc"]

    x0_d = nc.dram_tensor("x0", [Tc, E], F32, kind="ExternalInput")
    wq_d = nc.dram_tensor("wq", [E, E], F32, kind="ExternalInput")
    wk_d = nc.dram_tensor("wk", [E, E], F32, kind="ExternalInput")
    wv_d = nc.dram_tensor("wv", [E, E], F32, kind="ExternalInput")
    wo_d = nc.dram_tensor("wo", [E, E], F32, kind="ExternalInput")
    wg1_d = nc.dram_tensor("wg1", [E, Dh], F32, kind="ExternalInput")
    wg2_d = nc.dram_tensor("wg2", [Dh, E], F32, kind="ExternalInput")
    wgt_d = nc.dram_tensor("wgt", [E, INNER], F32, kind="ExternalInput")
    wut_d = nc.dram_tensor("wut", [E, INNER], F32, kind="ExternalInput")
    wdt_d = nc.dram_tensor("wdt", [INNER, E], F32, kind="ExternalInput")
    gab_d = nc.dram_tensor("gab", [128, 2 * E], F32, kind="ExternalInput")
    maskt_d = nc.dram_tensor("maskt", [H, CH, CH], F32, kind="ExternalInput")
    lamq_d = nc.dram_tensor("lamq", [CH, H], F32, kind="ExternalInput")
    lamkf_d = nc.dram_tensor("lamkf", [CH, H * Dh], F32, kind="ExternalInput")
    lamcf_d = nc.dram_tensor("lamcf", [64, H * Dh], F32, kind="ExternalInput")
    gamf_d = nc.dram_tensor("gamf", [n_st * CH, H * Dh], F32, kind="ExternalInput")
    cvec_d = nc.dram_tensor("cvec", [64, cfg.group * H], F32, kind="ExternalInput")

    wgq_d = nc.dram_tensor("wgq", [INNER // 512, 128, (E // 128) * 512], BF16)
    wuq_d = nc.dram_tensor("wuq", [INNER // 512, 128, (E // 128) * 512], BF16)
    wdq_d = nc.dram_tensor("wdq", [INNER // 256, 128, 2 * E], BF16)
    qsp_d = nc.dram_tensor("qsp", [Tc // 128, 64, H * 128], F32)
    ksp_d = nc.dram_tensor("ksp", [Tc // 128, 64, H * 128], F32)
    vsp_d = nc.dram_tensor("vsp", [Tc, E], F32)
    osp_d = nc.dram_tensor("osp", [Tc, E], F32)
    cin_d = nc.dram_tensor("cin", [64, H * Dh], F32)
    bnc_d = nc.dram_tensor("bnc", [1, 12], F32)
    cout_d = nc.dram_tensor("cout", [cfg.group * 64, H * Dh], F32)
    out_d = nc.dram_tensor("out", [Tc, E], F32, kind="ExternalOutput")
    if dbg:
        dkv_d = nc.dram_tensor("dkv", [64, H * Dh], F32, kind="ExternalOutput")
        doc_d = nc.dram_tensor("doc", [Tc, E], F32, kind="ExternalOutput")

    ctx = contextlib.ExitStack()
    with ctx:
        xr = ctx.enter_context(nc.sbuf_tensor("xr", [128, n_st, E], F32))
        idf = ctx.enter_context(nc.sbuf_tensor("idf", [128, 128], F32))
        idb = ctx.enter_context(nc.sbuf_tensor("idb", [128, 128], BF16))
        ind = ctx.enter_context(nc.sbuf_tensor("ind", [128, 1], F32))
        omg = ctx.enter_context(nc.sbuf_tensor("omg", [128, 1], F32))
        mb_ = ctx.enter_context(nc.sbuf_tensor("mb_", [128, 1], F32))
        thr = ctx.enter_context(nc.sbuf_tensor("thr", [128, 12], F32))
        st1 = ctx.enter_context(nc.sbuf_tensor("st1", [128, 16], F32))
        ones = ctx.enter_context(nc.sbuf_tensor("ones", [128, 4], F32))
        oneK = ctx.enter_context(nc.sbuf_tensor("oneK", [1, 128], F32))

        pA = ctx.enter_context(nc.psum_tensor("pA", [128, 512], F32))
        pB = ctx.enter_context(nc.psum_tensor("pB", [128, 512], F32))
        pC = ctx.enter_context(nc.psum_tensor("pC", [128, 512], F32))
        pD = ctx.enter_context(nc.psum_tensor("pD", [128, 512], F32))
        pE_ = ctx.enter_context(nc.psum_tensor("pE", [128, 512], F32))
        pF = ctx.enter_context(nc.psum_tensor("pF", [128, 512], F32))
        banks = [pA, pB, pC, pD, pE_, pF]

        P = Prog(nc, ctx)
        globals()["_LAST_P"] = P
        P.warm_ap = pE_[0:64, 0:128]
        P.warm_ops = (idb[:, 0:64], idb[:])

        with nc.sbuf_tensor("pidx", [128, 128], F32) as pidx:
            for t in range(n_st):
                P.ld(xr[:, t, :], x0_d.ap()[t * 128:(t + 1) * 128, :])
            P.iota(bass.AP(idf, 0, [[128, 128], [1, 128]]), [[1, 128]], 0)
            P.iota(bass.AP(pidx, 0, [[128, 128], [1, 128]]), [[0, 128]], 1)
            P.tt("vector", idf[:], idf[:], pidx[:, 0:128], ALU.is_equal)
            P.cp("vector", idb[:], idf[:])
            P.mset("vector", ind[:], 0.75)
            P.ts("vector", ind[:], ind[:], M2, ALU.add)
            P.ts("vector", ind[:], ind[:], -M2, ALU.add)
            P.ts("vector", omg[:], ind[:], -1.0, ALU.mult, 1.0, ALU.add)
            P.ts("vector", mb_[:], omg[:], -M1, ALU.mult)
            P.mset("vector", ones[:], 1.0)
            P.mset("vector", oneK[:], 1.0)
            P.phase("setup")

        with nc.sbuf_tensor("wsg", [128, 2, 4096], F32) as wsg, \
             nc.sbuf_tensor("acc3", [128, 8], F32) as acc3, \
             nc.sbuf_tensor("prt", [128, 4], F32) as prt, \
             nc.sbuf_tensor("thsrc", [1, 12], F32) as thsrc:
            P.mset("vector", acc3[:], 0.0)
            tiles = []
            for m, wd in enumerate((wgt_d, wut_d, wdt_d)):
                rows, cols = wd.shape
                if cols == 4096:
                    for r0 in range(0, rows, 128):
                        tiles.append((m, wd.ap()[r0:r0 + 128, :], 1))
                else:
                    for r0 in range(0, rows, 512):
                        tiles.append((m, wd.ap()[r0:r0 + 512, :].rearrange(
                            "(c p) o -> p c o", p=128), 4))
            for ti, (m, lap, rch) in enumerate(tiles):
                sl = ti % 2
                dst = wsg[:, sl, :] if rch == 1 else \
                    wsg[:, sl, :].rearrange("p (c o) -> p c o", c=rch)
                (P.ld if sl == 0 else P.sld)(dst, lap)
                if sl == 0:
                    P.act(wsg[:, sl, 0:4096], wsg[:, sl, 0:4096], ACTF.Abs,
                          accum=prt[:, 0:1])
                    P.tt("vector", acc3[:, m:m + 1], acc3[:, m:m + 1], prt[:, 0:1], ALU.add)
                else:
                    P.red("vector", prt[:, 1:2], wsg[:, sl, 0:4096], ALU.add, absval=True)
                    P.tt("vector", acc3[:, 4 + m:5 + m], acc3[:, 4 + m:5 + m], prt[:, 1:2], ALU.add)
            P.tt("vector", acc3[:, 0:3], acc3[:, 0:3], acc3[:, 4:7], ALU.add)
            P.mm(pA[0:4, 0:1], acc3[:, 0:4], ones[:, 0:1])
            nel = float(E * INNER)
            P.ts("vector", st1[0:4, 0:1], pA[0:4, 0:1], 1.0 / nel, ALU.mult, 1e-5, ALU.max)
            P.ts("vector", st1[0:4, 1:2], st1[0:4, 0:1], 0.5, ALU.mult)
            P.ts("vector", st1[0:4, 2:3], st1[0:4, 1:2], -1.0, ALU.mult)
            P.st(bnc_d.ap()[0:1, 0:4], st1[0:4, 0:1])
            P.st(bnc_d.ap()[0:1, 4:8], st1[0:4, 1:2])
            P.st(bnc_d.ap()[0:1, 8:12], st1[0:4, 2:3])
            P.ld(thsrc[0:1, 0:12], bnc_d.ap())
            P.mm(pA[:, 0:12], oneK[:], thsrc[:])
            P.cp("vector", thr[:, 0:3], pA[:, 0:3])
            P.cp("vector", thr[:, 3:6], pA[:, 4:7])
            P.cp("vector", thr[:, 6:9], pA[:, 8:11])
            P.phase("wscale")

        with nc.sbuf_tensor("qw_s", [128, 2, 4096], F32) as qw_s, \
             nc.sbuf_tensor("qo", [128, 2, 4096], BF16) as qo, \
             nc.sbuf_tensor("qt1", [128, 4096], BF16) as qt1, \
             nc.sbuf_tensor("qt2", [128, 4096], BF16) as qt2, \
             nc.sbuf_tensor("qt3", [128, 4096], BF16) as qt3, \
             nc.sbuf_tensor("qt4", [128, 4096], BF16) as qt4:
            tiles = []
            for m2, (wd, scr, thix) in enumerate(((wgt_d, wgq_d, 3), (wut_d, wuq_d, 4))):
                for c in range(E // 128):
                    tiles.append((wd.ap()[c * 128:(c + 1) * 128, :],
                                  scr.ap().rearrange("n p o -> p n o")[:, :, c * 512:(c + 1) * 512],
                                  thix, 1))
            for t in range(INNER // 512):
                tiles.append((wdt_d.ap()[t * 512:(t + 1) * 512, :].rearrange(
                    "(c p) o -> p c o", p=128),
                    wdq_d.ap()[2 * t:2 * t + 2].rearrange("g p o -> p g o"),
                    5, 4))
            for ti, (lap, sap, thix, rch) in enumerate(tiles):
                sl = ti % 2
                dst = qw_s[:, sl, :] if rch == 1 else \
                    qw_s[:, sl, :].rearrange("p (c o) -> p c o", c=rch)
                P.ld(dst, lap)
                w_ = qw_s[:, sl, 0:4096]
                o_ = qo[:, sl, 0:4096]
                if ti % 2 == 0:
                    P.ts("vector", qt1[:, :], w_, thr[:, thix:thix + 1], ALU.is_ge)
                    P.ts("vector", qt2[:, :], w_, -1.0, ALU.mult, thr[:, thix:thix + 1], ALU.is_ge)
                    P.tt("vector", o_, qt1[:, :], qt2[:, :], ALU.subtract)
                else:
                    P.act(qt3[:, :], w_, ACTF.Relu, bias=thr[:, thix + 3:thix + 4])
                    P.act(qt4[:, :], w_, ACTF.Relu, bias=thr[:, thix + 3:thix + 4], scale=-1.0)
                    P.tt("vector", qt3[:, :], qt3[:, :], qt4[:, :], ALU.subtract)
                    P.act(o_, qt3[:, :], ACTF.Sign)
                P.gld(sap, o_)
            P.dma_fence()
            P.phase("wquant")

        def _sl2(ap3, lo, hi):
            return ap3[:, lo:hi] if len(ap3.shape) == 2 else ap3[:, :, lo:hi]

        def quant_round(dst, src, scale_ap, t1, t2, width=None, center=None):
            n = width if width is not None else src.shape[-1]
            for i, lo in enumerate(range(0, n, 512)):
                hi = min(lo + 512, n)
                w = hi - lo
                q = i % t1.shape[1]
                q2 = i % t2.shape[1]
                s_ = _sl2(src, lo, hi)
                d_ = _sl2(dst, lo, hi)
                P.warm()
                if center is not None:
                    P.ts("vector", t2[:, q2, 0:w], s_, scale_ap, ALU.mult, center, ALU.add)
                    P.act(t1[:, q, 0:w], t2[:, q2, 0:w], ACTF.Identity, bias=M1)
                else:
                    P.act(t1[:, q, 0:w], s_, ACTF.Identity, bias=M1, scale=scale_ap)
                P.ts("vector", t2[:, q2, 0:w], t1[:, q, 0:w], DM, ALU.add)
                P.act(t1[:, q, 0:w], t1[:, q, 0:w], ACTF.Identity, bias=mb_[:], scale=omg[:])
                P.stt("vector", d_, t2[:, q2, 0:w], -M2, t1[:, q, 0:w], ALU.add, ALU.add)

        def ternarize(dst, src, thr_ap, t1, t2):
            n = src.shape[-1]
            dn = dst.shape[-1]
            for lo in range(0, n, 512):
                hi = min(lo + 512, n)
                w = hi - lo
                s_ = _sl2(src, lo, hi)
                if dn == n or len(dst.shape) == 2:
                    d_ = _sl2(dst, lo, hi)
                else:
                    o = dst.shape[-1]
                    assert lo % o == 0 and hi % o == 0
                    d_ = dst[:, lo // o:hi // o, :]
                q = (lo // 512) % 2
                P.warm()
                P.ts("vector", t1[:, q, 0:w], s_, thr_ap, ALU.is_ge)
                P.ts("vector", t2[:, q, 0:w], s_, -1.0, ALU.mult)
                P.ts("vector", t2[:, q, 0:w], t2[:, q, 0:w], thr_ap, ALU.is_ge)
                P.tt("vector", d_, t1[:, q, 0:w], t2[:, q, 0:w], ALU.subtract)

        def emit_ffn(first):
            fx = "a" if first else "b"
            zz = contextlib.ExitStack()
            with zz:
                _saved_warm = P.warm_ap
                P.warm_ap = None
                zz.callback(lambda: setattr(P, "warm_ap", _saved_warm))
                gu = zz.enter_context(nc.sbuf_tensor(f"gu{fx}", [128, SUP, INNER], F32))
                xqt = zz.enter_context(nc.sbuf_tensor(f"xqt{fx}", [128, 2, NOC, SUP * 128], BF16))
                xq2 = zz.enter_context(nc.sbuf_tensor(f"xq2{fx}", [128, 2, 1024], BF16))
                hqt = zz.enter_context(nc.sbuf_tensor(f"hqt{fx}", [128, 2, NOCI * 128], BF16))
                wgr = zz.enter_context(nc.sbuf_tensor(f"wgr{fx}", [128, 2, NOC, 512], BF16))
                wur = zz.enter_context(nc.sbuf_tensor(f"wur{fx}", [128, 1, NOC, 512], BF16))
                wdr = zz.enter_context(nc.sbuf_tensor(f"wdr{fx}", [128, 3, 2, E], BF16))
                tq1 = zz.enter_context(nc.sbuf_tensor(f"tq1{fx}", [128, 2, 512], F32))
                tq2 = zz.enter_context(nc.sbuf_tensor(f"tq2{fx}", [128, 1, 512], F32))
                fst = zz.enter_context(nc.sbuf_tensor(f"fst{fx}", [128, 2, SUP, 8], F32))
                pTa_g = nc.psum_tensor(f"pTa{fx}", [128, 1024], BF16)
                pTb_g = nc.psum_tensor(f"pTb{fx}", [128, 1024], BF16)
                pT2_ = [pTa_g.__enter__(), pTb_g.__enter__()]

                def stepA_chain(sp, u):
                    if True:
                        fsl = sp % 2
                        t = sp * SUP + u
                        xt = xr[:, t, :]
                        P.red("vector", st1[:, 0:1], xt, ALU.add)
                        P.ts("vector", fst[:, fsl, u, 0:1], st1[:, 0:1], -1.0 / E, ALU.mult)
                        P.warm()
                        P.act(xq2[:, u % 2, 0:E], xt, ACTF.Square,
                              bias=fst[:, fsl, u, 0:1], accum=st1[:, 1:2])
                        P.act(st1[:, 2:3], st1[:, 1:2], ACTF.Sqrt, bias=1e-5, scale=1.0 / E)
                        P.recip(fst[:, fsl, u, 1:2], st1[:, 2:3])
                        P.red("vector", st1[:, 3:4], xt, ALU.max)
                        P.red("vector", st1[:, 13:14], xt, ALU.min)
                        P.tt("vector", st1[:, 3:4], st1[:, 3:4], fst[:, fsl, u, 0:1], ALU.add)
                        P.tt("vector", st1[:, 13:14], st1[:, 13:14], fst[:, fsl, u, 0:1], ALU.add)
                        P.ts("vector", st1[:, 13:14], st1[:, 13:14], -1.0, ALU.mult)
                        P.tt("vector", st1[:, 3:4], st1[:, 3:4], st1[:, 13:14], ALU.max)
                        P.warm()
                        P.tt("vector", st1[:, 4:5], st1[:, 3:4], fst[:, fsl, u, 1:2], ALU.mult)
                        P.ts("vector", st1[:, 5:6], st1[:, 4:5], 1e-5, ALU.max)
                        P.recip(st1[:, 6:7], st1[:, 5:6])
                        P.ts("vector", st1[:, 7:8], st1[:, 6:7], 127.0, ALU.mult)
                        P.ts("vector", fst[:, fsl, u, 3:4], st1[:, 5:6], thr[:, 0:1], ALU.mult, 1.0 / 127.0, ALU.mult)
                        P.ts("vector", fst[:, fsl, u, 4:5], st1[:, 5:6], thr[:, 1:2], ALU.mult, 1.0 / 127.0, ALU.mult)
                        P.tt("vector", st1[:, 8:9], fst[:, fsl, u, 1:2], st1[:, 7:8], ALU.mult)
                        P.tt("vector", st1[:, 14:15], fst[:, fsl, u, 0:1], st1[:, 8:9],
                             ALU.mult)
                        quant_round(xq2[:, u % 2, 0:E], xt, st1[:, 8:9], tq1, tq2,
                                    center=st1[:, 14:15])

                def stepA_trans(sp, u):
                    fsl = sp % 2
                    pt = pT2_[u % 2]
                    for ich in range(NOC):
                        P.tr(pt[:, ich * 128:(ich + 1) * 128],
                             xq2[:, u % 2, ich * 128:(ich + 1) * 128], idb[:])
                    P.cp("vector", xqt[:, sp % 2, 0:NOC, u * 128:(u + 1) * 128], pt[:, 0:1024])
                    P.mset("vector", fst[:, fsl, u, 5:6], 0.0)

                def stepA_stile(sp, u):
                    stepA_chain(sp, u)
                    stepA_trans(sp, u)

                def stepA(sp):
                    for u in range(SUP):
                        stepA_stile(sp, u)

                def stepB(sp, nxt=None):
                    fsl = sp % 2
                    xs = sp % 2
                    for oc in range(NO5):
                        if nxt and oc % 2 == 1 and (oc // 2) < SUP:
                            u_ = oc // 2
                            if u_ > 0:
                                stepA_trans(sp + 1, u_ - 1)
                            stepA_chain(sp + 1, u_)
                        wv_ = oc % 2
                        P.ld(wgr[:, wv_, :, :], wgq_d.ap()[oc])
                        P.sld(wur[:, 0, :, :], wuq_d.ap()[oc])
                        for u in range(SUP):
                            P.mmg(banks[u][:, 0:512], [(xqt[:, xs, i, u * 128:(u + 1) * 128], wur[:, 0, i, :]) for i in range(NOC)])
                        for u in range(SUP):
                            pg = banks[4 + (u % 2)]
                            P.mmg(pg[:, 0:512], [(xqt[:, xs, i, u * 128:(u + 1) * 128], wgr[:, wv_, i, :]) for i in range(NOC)])
                            guc = gu[:, u, oc * 512:(oc + 1) * 512]
                            P.act(guc, pg[:, 0:512], ACTF.Silu, scale=fst[:, fsl, u, 3:4])
                            P.stt("vector", guc, banks[u][:, 0:512], fst[:, fsl, u, 4:5], guc, ALU.mult, ALU.mult)
                            P.red("vector", st1[:, 9:10], guc, ALU.max, absval=True)
                            P.tt("vector", fst[:, fsl, u, 5:6], fst[:, fsl, u, 5:6], st1[:, 9:10], ALU.max)

                def c_group(sp, up, ui):
                    fsl = sp % 2
                    u = up * 2 + ui
                    P.ts("vector", st1[:, 10:11], fst[:, fsl, u, 5:6], 1e-5, ALU.max)
                    P.recip(st1[:, 11:12], st1[:, 10:11])
                    P.ts("vector", fst[:, fsl, u, 6:7], st1[:, 11:12], 127.0, ALU.mult)
                    P.ts("vector", fst[:, fsl, u, 7:8], st1[:, 10:11], thr[:, 2:3], ALU.mult, 0.5 / 127.0, ALU.mult)
                    for sl in range(INNER // 1024):
                        quant_round(xq2[:, sl % 2, 0:1024],
                                    gu[:, u, sl * 1024:(sl + 1) * 1024],
                                    fst[:, fsl, u, 6:7], tq1, tq2)
                        pt = pT2_[sl % 2]
                        for k in range(8):
                            P.tr(pt[:, k * 128:(k + 1) * 128],
                                 xq2[:, sl % 2, k * 128:(k + 1) * 128], idb[:])
                        P.cp("vector", hqt[:, ui, sl * 1024:(sl + 1) * 1024], pt[:, 0:1024])

                def stepC(sp, last, nxt):
                    fsl = sp % 2
                    if nxt:
                        stepA_trans(sp + 1, SUP - 1)
                    for up in range(SUP // 2):
                        c_group(sp, up, 0)
                        c_group(sp, up, 1)
                        for g2 in range(NOCI // 2):
                            slot = g2 % 3
                            P.ld(wdr[:, slot, :, :], wdq_d.ap()[g2])
                            for j in range(2):
                                och = g2 * 2 + j
                                for ui in range(2):
                                    for eh in range(2):
                                        P.mm(banks[2 * ui + eh][:, 0:512],
                                             hqt[:, ui, och * 128:(och + 1) * 128],
                                             wdr[:, slot, j, eh * 512:(eh + 1) * 512],
                                             start=(och == 0), stop=(och == NOCI - 1))
                        for ui in range(2):
                            u = up * 2 + ui
                            t = sp * SUP + u
                            P.stt("vector", xr[:, t, 0:512], banks[2 * ui][:, 0:512], fst[:, fsl, u, 7:8],
                                  xr[:, t, 0:512], ALU.mult, ALU.add)
                            P.stt("vector", xr[:, t, 512:1024], banks[2 * ui + 1][:, 0:512], fst[:, fsl, u, 7:8],
                                  xr[:, t, 512:1024], ALU.mult, ALU.add)
                    if last:
                        P.phase(f"ffn{fx}")

                stepA(0)
                for sp in range(n_sup):
                    nx = sp + 1 < n_sup
                    stepB(sp, nxt=nx)
                    stepC(sp, last=(sp == n_sup - 1), nxt=nx)
                pTb_g.__exit__(None, None, None)
                pTa_g.__exit__(None, None, None)

        emit_ffn(first=True)

        za = contextlib.ExitStack()
        with za:
            lamq_s = za.enter_context(nc.sbuf_tensor("lamq_s", [CH, H], F32))
            kv_a = za.enter_context(nc.sbuf_tensor("kv_a", [64, H * Dh], F32))
            g1f = za.enter_context(nc.sbuf_tensor("g1f", [64, Tc], F32))
            g1h = za.enter_context(nc.sbuf_tensor("g1h", [64, Tc], BF16))
            g1l = za.enter_context(nc.sbuf_tensor("g1l", [64, Tc], BF16))
            kv_b = za.enter_context(nc.sbuf_tensor("kv_b", [64, H * Dh], F32))

            zb3 = contextlib.ExitStack()
            with zb3:
                h2h = zb3.enter_context(nc.sbuf_tensor("h2h", [128, NOC, Tc], BF16))
                h2l = zb3.enter_context(nc.sbuf_tensor("h2l", [128, NOC, Tc], BF16))
                tmh = zb3.enter_context(nc.sbuf_tensor("tmh", [128, E], BF16))
                tml = zb3.enter_context(nc.sbuf_tensor("tml", [128, E], BF16))
                dq2 = zb3.enter_context(nc.sbuf_tensor("dq2", [128, E], F32))
                pTh = zb3.enter_context(nc.psum_tensor("pTh", [128, 1024], BF16))
                pTl = zb3.enter_context(nc.psum_tensor("pTl", [128, 1024], BF16))
                P.ld(lamq_s.ap(), lamq_d.ap())
                for t in range(n_st):
                    xt = xr[:, t, :]
                    P.warm()
                    P.act(dq2[:], xt, ACTF.Identity, accum=st1[:, 0:1])
                    P.ts("vector", st1[:, 1:2], st1[:, 0:1], 1.0 / E, ALU.mult)
                    P.ts("vector", dq2[:], xt, st1[:, 1:2], ALU.subtract)
                    P.act(tmh.ap(), dq2[:], ACTF.Square, accum=st1[:, 2:3])
                    P.act(st1[:, 3:4], st1[:, 2:3], ACTF.Sqrt, bias=1e-5, scale=1.0 / E)
                    P.recip(st1[:, 4:5], st1[:, 3:4])
                    P.ts("vector", dq2[:], dq2[:], st1[:, 4:5], ALU.mult)
                    P.cp("vector", tmh[:], dq2[:])
                    P.tt("vector", tml[:], dq2[:], tmh[:], ALU.subtract)
                    for ich in range(NOC):
                        P.tr(pTh[:, ich * 128:(ich + 1) * 128], tmh[:, ich * 128:(ich + 1) * 128], idb[:])
                        P.tr(pTl[:, ich * 128:(ich + 1) * 128], tml[:, ich * 128:(ich + 1) * 128], idb[:])
                    P.cp("vector", h2h[:, 0:NOC, t * 128:(t + 1) * 128], pTh[:, 0:1024])
                    P.cp("vector", h2l[:, 0:NOC, t * 128:(t + 1) * 128], pTl[:, 0:1024])

                zw = contextlib.ExitStack()
                with zw:
                    wkf = zw.enter_context(nc.sbuf_tensor("wkf", [128, 2, E], F32))
                    wkh = zw.enter_context(nc.sbuf_tensor("wkh", [128, NOC, E], BF16))
                    wkl = zw.enter_context(nc.sbuf_tensor("wkl", [128, NOC, E], BF16))
                    stg = zw.enter_context(nc.sbuf_tensor("stg", [128, 3, 512], F32))

                    def prep_w(wd, ncol):
                        for c in range(NOC):
                            sl = c % 2
                            P.ld(wkf[:, sl, 0:ncol], wd.ap()[c * 128:(c + 1) * 128, 0:ncol])
                            P.gld(wkh[:, c, 0:ncol], wkf[:, sl, 0:ncol])
                            P.tt("vector", wkl[:, c, 0:ncol], wkf[:, sl, 0:ncol],
                                 wkh[:, c, 0:ncol], ALU.subtract)

                    for mi, (wd, sp_d) in enumerate(((wq_d, qsp_d), (wk_d, ksp_d))):
                        prep_w(wd, E)
                        for oc in range(NOC):
                            for sc in range(NS5):
                                ss = slice(sc * SC, (sc + 1) * SC)
                                pairs = []
                                for i in range(NOC):
                                    wh = wkh[:, i, oc * 128:(oc + 1) * 128]
                                    wl = wkl[:, i, oc * 128:(oc + 1) * 128]
                                    pairs += [(wh, h2h[:, i, ss]), (wh, h2l[:, i, ss]), (wl, h2h[:, i, ss])]
                                pp = pA if sc % 2 == 0 else pB
                                P.mmg(pp[:, 0:SC], pairs)
                                ps = sc % 3
                                P.act(stg[:, ps, 0:SC], pp[:, 0:SC], ACTF.Silu)
                                ap4 = sp_d.ap().rearrange("c d (h t) -> h d c t", h=H)
                                for b_ in range(2):
                                    P.st(ap4[2 * oc + b_, :, sc * 4:(sc + 1) * 4, :],
                                         stg[b_ * 64:(b_ + 1) * 64, ps, 0:SC])
                    prep_w(wv_d, E)
                    for oc2 in range(E // 512):
                        cs = slice(oc2 * 512, (oc2 + 1) * 512)
                        for t in range(n_st):
                            ts_ = slice(t * 128, (t + 1) * 128)
                            pairs = []
                            for i in range(NOC):
                                pairs += [(h2h[:, i, ts_], wkh[:, i, cs]),
                                          (h2h[:, i, ts_], wkl[:, i, cs]),
                                          (h2l[:, i, ts_], wkh[:, i, cs])]
                            pp = pC if t % 2 == 0 else pD
                            P.mmg(pp[:, 0:512], pairs)
                            ps = t % 3
                            P.act(stg[:, ps, 0:512], pp[:, 0:512], ACTF.Silu)
                            P.st(vsp_d.ap()[ts_, oc2 * 512:(oc2 + 1) * 512], stg[:, ps, 0:512])
                    for i in range(NOC):
                        sl = i % 2
                        P.ld(wkf[:, sl, 0:Dh], wg1_d.ap()[i * 128:(i + 1) * 128, :])
                        P.gld(wkh[:, i, 0:Dh], wkf[:, sl, 0:Dh])
                        P.tt("vector", wkl[:, i, 0:Dh], wkf[:, sl, 0:Dh],
                             wkh[:, i, 0:Dh], ALU.subtract)
                    for sc in range(NS5):
                        ss = slice(sc * SC, (sc + 1) * SC)
                        pairs = []
                        for i in range(NOC):
                            wh = wkh[:, i, 0:Dh]
                            wl = wkl[:, i, 0:Dh]
                            pairs += [(wh, h2h[:, i, ss]), (wh, h2l[:, i, ss]), (wl, h2h[:, i, ss])]
                        pp = pA if sc % 2 == 0 else pB
                        P.mmg(pp[0:64, 0:SC], pairs)
                        P.cp("vector", g1f[:, ss], pp[0:64, 0:SC])
                    P.cp("vector", g1h[:], g1f[:])
                    P.tt("vector", g1l[:], g1f[:], g1h[:], ALU.subtract)
                    P.dma_fence()
                    P.phase("proj")

            zi = contextlib.ExitStack()
            with zi:
                lamkf_s = zi.enter_context(nc.sbuf_tensor("lamkf_s", [CH, H * Dh], F32))
                lamcf_s = zi.enter_context(nc.sbuf_tensor("lamcf_s", [64, H * Dh], F32))
                mask_s = zi.enter_context(nc.sbuf_tensor("mask_s", [CH, H, CH], F32))
                cv_s = zi.enter_context(nc.sbuf_tensor("cv_s", [64, cfg.group * H], F32))
                kf = zi.enter_context(nc.sbuf_tensor("kf", [64, 2, H * CH], F32))
                qf = zi.enter_context(nc.sbuf_tensor("qf", [64, 2, H * CH], F32))
                vt = zi.enter_context(nc.sbuf_tensor("vt", [128, 2, E], F32))
                vl = zi.enter_context(nc.sbuf_tensor("vl", [128, 2, E], F32))
                kt = zi.enter_context(nc.sbuf_tensor("kt", [128, 2, E], F32))
                smt = zi.enter_context(nc.sbuf_tensor("smt", [128, 2, H * CH], F32))
                toi = zi.enter_context(nc.sbuf_tensor("toi", [128, 2, E], F32))
                kvt = zi.enter_context(nc.sbuf_tensor("kvt", [64, 2, E], F32))
                och = zi.enter_context(nc.sbuf_tensor("och", [128, 2, E], F32))
                pW = zi.enter_context(nc.psum_tensor("pW", [128, 512], F32))
                P.warm_ap = pW[0:64, 0:128]
                P.ld(lamkf_s.ap(), lamkf_d.ap())
                P.ld(lamcf_s.ap(), lamcf_d.ap())
                P.ld(cv_s.ap(), cvec_d.ap())
                P.ld(mask_s.ap(), maskt_d.ap().rearrange("h p c -> p h c"))
                P.mset("vector", kv_a[:], 0.0)
                for c in range(n_st):
                    cs = slice(c * 128, (c + 1) * 128)
                    sl = c % 2
                    kv_cur = kv_a if c % 2 == 0 else kv_b
                    kv_nxt = kv_b if c % 2 == 0 else kv_a
                    P.ld(kf[:, sl, :], ksp_d.ap()[c])
                    P.ld(qf[:, sl, :], qsp_d.ap()[c])
                    P.ld(vt[:, sl, :], vsp_d.ap()[cs, :])
                    for h in range(H):
                        P.mm(banks[h // 4][:, (h % 4) * 128:(h % 4 + 1) * 128],
                             kf[:, sl, h * CH:(h + 1) * CH], qf[:, sl, h * CH:(h + 1) * CH])
                    P.warm()
                    for b4 in range(4):
                        P.tt("vector", smt[:, sl, b4 * 512:(b4 + 1) * 512], banks[b4][:, 0:512],
                             mask_s[:, 4 * b4:4 * (b4 + 1), :], ALU.mult)
                    for h in range(H):
                        pk = pE_ if h < 8 else pF
                        P.tr(pk[:, (h % 8) * 64:(h % 8 + 1) * 64],
                             kf[:, sl, h * CH:(h + 1) * CH], idf[0:64, 0:64])
                    P.cp("vector", kt[:, sl, 0:512], pE_[:, 0:512])
                    P.cp("vector", kt[:, sl, 512:1024], pF[:, 0:512])
                    P.warm()
                    P.tt("vector", vl[:, sl, :], vt[:, sl, :], lamkf_s[:], ALU.mult)
                    for h in range(H):
                        ds = slice(h * Dh, (h + 1) * Dh)
                        pi = banks[0] if h < 8 else banks[1]
                        P.mm(pi[:, (h % 8) * 64:(h % 8 + 1) * 64], smt[:, sl, h * CH:(h + 1) * CH],
                             vt[:, sl, ds])
                    for h in range(H):
                        ds = slice(h * Dh, (h + 1) * Dh)
                        po = banks[2] if h < 8 else banks[3]
                        P.mm(po[:, (h % 8) * 64:(h % 8 + 1) * 64], qf[:, sl, h * CH:(h + 1) * CH],
                             kv_cur[:, ds])
                    for h in range(H):
                        ds = slice(h * Dh, (h + 1) * Dh)
                        pa = pE_ if h < 8 else pF
                        P.mm(pa[0:64, (h % 8) * 64:(h % 8 + 1) * 64], kt[:, sl, ds], vl[:, sl, ds])
                    P.warm()
                    for h in range(H):
                        po = banks[2] if h < 8 else banks[3]
                        P.act(toi[:, sl, h * Dh:(h + 1) * Dh], po[:, (h % 8) * 64:(h % 8 + 1) * 64],
                              ACTF.Identity, scale=lamq_s[:, h:h + 1])
                    P.tt("vector", och[:, sl, 0:512], toi[:, sl, 0:512], banks[0][:, 0:512], ALU.add)
                    P.tt("vector", och[:, sl, 512:1024], toi[:, sl, 512:1024], banks[1][:, 0:512], ALU.add)
                    P.warm()
                    P.tt("vector", kvt[:, sl, :], kv_cur[:], lamcf_s[:], ALU.mult)
                    P.tt("vector", kv_nxt[:, 0:512], kvt[:, sl, 0:512], pE_[0:64, 0:512], ALU.add)
                    P.tt("vector", kv_nxt[:, 512:1024], kvt[:, sl, 512:1024], pF[0:64, 0:512], ALU.add)
                    P.warm()
                    P.st(osp_d.ap()[cs, :], och[:, sl, :])
                P.phase("scan")
                kv_fin = kv_a if n_st % 2 == 0 else kv_b
                P.st(cin_d.ap(), kv_fin.ap())
                groups = [list(range(g * cfg.group, (g + 1) * cfg.group))
                          for g in range(cfg.n_cores // cfg.group)]
                P.cc("AllGather", ALU.bypass, groups, cin_d.ap().opt(), cout_d.ap().opt())
                P.mset("vector", kv_a[:], 0.0)
                for j in range(cfg.group):
                    P.ld(kv_b.ap(), cout_d.ap()[j * 64:(j + 1) * 64, :])
                    for h in range(H):
                        ds = slice(h * Dh, (h + 1) * Dh)
                        P.stt("vector", kv_a[:, ds], kv_b[:, ds],
                              cv_s[:, j * H + h:j * H + h + 1], kv_a[:, ds], ALU.mult, ALU.add)
                if dbg:
                    P.st(dkv_d.ap(), kv_a.ap())
                P.dma_fence()
                P.phase("exchange")

            zb6 = contextlib.ExitStack()
            with zb6:
                qf6 = zb6.enter_context(nc.sbuf_tensor("qf6", [64, 2, H * CH], F32))
                och6 = zb6.enter_context(nc.sbuf_tensor("och6", [128, 2, E], F32))
                gmt = zb6.enter_context(nc.sbuf_tensor("gmt", [128, 2, E], F32))
                cor = zb6.enter_context(nc.sbuf_tensor("cor", [128, 1, E], F32))
                gat = zb6.enter_context(nc.sbuf_tensor("gat", [128, 1, E], F32))
                yv = zb6.enter_context(nc.sbuf_tensor("yv", [128, 1, E], F32))
                yh = zb6.enter_context(nc.sbuf_tensor("yh", [128, 1, E], BF16))
                yl = zb6.enter_context(nc.sbuf_tensor("yl", [128, 1, E], BF16))
                yth = zb6.enter_context(nc.sbuf_tensor("yth", [128, 1, NOC, 128], BF16))
                ytl = zb6.enter_context(nc.sbuf_tensor("ytl", [128, 1, NOC, 128], BF16))
                gab_s = zb6.enter_context(nc.sbuf_tensor("gab_s", [128, 2 * E], F32))
                woh = zb6.enter_context(nc.sbuf_tensor("woh", [128, NOC, E], BF16))
                wol = zb6.enter_context(nc.sbuf_tensor("wol", [128, NOC, E], BF16))
                wsf6 = zb6.enter_context(nc.sbuf_tensor("wsf6", [128, 2, E], F32))
                g2h = zb6.enter_context(nc.sbuf_tensor("g2h", [64, E], BF16))
                g2l = zb6.enter_context(nc.sbuf_tensor("g2l", [64, E], BF16))
                pTy = zb6.enter_context(nc.psum_tensor("pTy", [128, 1024], BF16))
                pW6 = zb6.enter_context(nc.psum_tensor("pW6", [128, 512], F32))
                P.warm_ap = pW6[0:64, 0:128]

                P.ld(gab_s.ap(), gab_d.ap())
                P.ld(wsf6[0:64, 0, :], wg2_d.ap())
                P.gld(g2h.ap(), wsf6[0:64, 0, :])
                P.tt("vector", g2l[:], wsf6[0:64, 0, :], g2h[:], ALU.subtract)
                for i in range(NOC):
                    wl_ = i % 2
                    P.ld(wsf6[:, wl_, :], wo_d.ap()[i * 128:(i + 1) * 128, :])
                    P.gld(woh[:, i, :], wsf6[:, wl_, :])
                    P.tt("vector", wol[:, i, :], wsf6[:, wl_, :], woh[:, i, :], ALU.subtract)
                for c in range(n_st):
                    cs = slice(c * 128, (c + 1) * 128)
                    sl = c % 2
                    P.ld(qf6[:, sl, :], qsp_d.ap()[c])
                    P.ld(och6[:, sl, :], osp_d.ap()[cs, :])
                    P.ld(gmt[:, sl, :], gamf_d.ap()[cs, :])
                    P.warm()
                    for h in range(H):
                        ds = slice(h * Dh, (h + 1) * Dh)
                        pc_ = pA if h < 8 else pB
                        P.mm(pc_[:, (h % 8) * 64:(h % 8 + 1) * 64], qf6[:, sl, h * CH:(h + 1) * CH],
                             kv_a[:, ds])
                    P.tt("vector", cor[:, 0, 0:512], pA[:, 0:512], gmt[:, sl, 0:512], ALU.mult)
                    P.tt("vector", cor[:, 0, 512:1024], pB[:, 0:512], gmt[:, sl, 512:1024], ALU.mult)
                    P.tt("vector", och6[:, sl, :], och6[:, sl, :], cor[:, 0, :], ALU.add)
                    if dbg:
                        P.st(doc_d.ap()[cs, :], och6[:, sl, :])
                    P.warm()
                    P.act(yv[:, 0, :], och6[:, sl, :], ACTF.Identity, accum=st1[:, 0:1])
                    P.ts("vector", st1[:, 1:2], st1[:, 0:1], 1.0 / E, ALU.mult)
                    P.ts("vector", yv[:, 0, :], och6[:, sl, :], st1[:, 1:2], ALU.subtract)
                    P.act(gat[:, 0, 0:E], yv[:, 0, :], ACTF.Square, accum=st1[:, 2:3])
                    P.act(st1[:, 3:4], st1[:, 2:3], ACTF.Sqrt, bias=1e-5, scale=1.0 / E)
                    P.recip(st1[:, 4:5], st1[:, 3:4])
                    P.ts("vector", yv[:, 0, :], yv[:, 0, :], st1[:, 4:5], ALU.mult)
                    P.tt("vector", yv[:, 0, :], yv[:, 0, :], gab_s[:, 0:E], ALU.mult)
                    P.tt("vector", yv[:, 0, :], yv[:, 0, :], gab_s[:, E:2 * E], ALU.add)
                    P.warm()
                    gbk = (pC, pD) if sl == 0 else (pE_, pF)
                    for e2 in range(E // 512):
                        es = slice(e2 * 512, (e2 + 1) * 512)
                        pp = gbk[e2]
                        P.mmg(pp[:, 0:512], [(g1h[:, cs], g2h[:, es]),
                                             (g1h[:, cs], g2l[:, es]),
                                             (g1l[:, cs], g2h[:, es])])
                        P.act(gat[:, 0, es], pp[:, 0:512], ACTF.Sigmoid)
                    P.warm()
                    P.tt("vector", yv[:, 0, :], yv[:, 0, :], gat[:, 0, :], ALU.mult)
                    P.cp("vector", yh[:, 0, :], yv[:, 0, :])
                    P.tt("vector", yl[:, 0, :], yv[:, 0, :], yh[:, 0, :], ALU.subtract)
                    for i in range(NOC):
                        P.tr(pTy[:, i * 128:(i + 1) * 128], yh[:, 0, i * 128:(i + 1) * 128], idb[:])
                    P.cp("vector", yth[:, 0, :, :].rearrange("p c o -> p (c o)"), pTy[:, 0:1024])
                    for i in range(NOC):
                        P.tr(pTy[:, i * 128:(i + 1) * 128], yl[:, 0, i * 128:(i + 1) * 128], idb[:])
                    P.cp("vector", ytl[:, 0, :, :].rearrange("p c o -> p (c o)"), pTy[:, 0:1024])
                    for e2 in range(E // 512):
                        es = slice(e2 * 512, (e2 + 1) * 512)
                        pairs = []
                        for i in range(NOC):
                            pairs += [(yth[:, 0, i, :], woh[:, i, es]),
                                      (yth[:, 0, i, :], wol[:, i, es]),
                                      (ytl[:, 0, i, :], woh[:, i, es])]
                        pp = ((pE_, pF) if sl == 0 else (pC, pD))[e2]
                        P.mmg(pp[:, 0:512], pairs)
                        P.tt("vector", xr[:, c, es], xr[:, c, es], pp[:, 0:512], ALU.add)
                P.phase("fin")
                P.dma_fence()
                P.phase("fin_end")

        P.warm_ap = pE_[0:64, 0:128]
        emit_ffn(first=False)

        for t in range(n_st):
            P.st(out_d.ap()[t * 128:(t + 1) * 128, :], xr[:, t, :], queue="fin")
        P.wait_all_dma("sync")
        P.wait_all_dma("gpsimd")
        P.phase("store")

    return nc


def core_inputs(cfg, core, full):
    tabs = host_tables(cfg)
    pos = core % cfg.group
    E = cfg.E
    xf = np.ascontiguousarray(full["x"].reshape(-1, E))
    sl = slice(core * cfg.Tc, (core + 1) * cfg.Tc)
    gab = np.concatenate([
        np.broadcast_to(full["gamma"].reshape(1, E), (128, E)),
        np.broadcast_to(full["beta"].reshape(1, E), (128, E))], axis=1)
    return dict(
        x0=np.ascontiguousarray(xf[sl]),
        wq=np.ascontiguousarray(full["Wq"]),
        wk=np.ascontiguousarray(full["Wk"]),
        wv=np.ascontiguousarray(full["Wv"]),
        wo=np.ascontiguousarray(full["Wo"]),
        wg1=np.ascontiguousarray(full["Wg1"]),
        wg2=np.ascontiguousarray(full["Wg2"]),
        wgt=np.ascontiguousarray(full["w_gate"].T),
        wut=np.ascontiguousarray(full["w_up"].T),
        wdt=np.ascontiguousarray(full["w_down"].T),
        gab=np.ascontiguousarray(gab),
        maskt=tabs["maskT"],
        lamq=tabs["lamq"],
        lamkf=tabs["lamk_full"],
        lamcf=tabs["lamc_full"],
        gamf=tabs["gam_full"],
        cvec=host_cvec(cfg, pos),
    )


from concourse.bass_utils import run_bass_kernel_spmd

_NC_CACHE = {}


def kernel(L0_attn_x, L0_attn_Wq, L0_attn_Wk, L0_attn_Wv, L0_attn_Wo,
           L0_attn_gamma, L0_attn_beta, L0_attn_Wg1, L0_attn_Wg2,
           L0_ffn_w_gate, L0_ffn_w_up, L0_ffn_w_down):
    cfg = CFG(Tc=2048, n_cores=8, group=4, sup=4)
    full = dict(
        x=np.asarray(L0_attn_x, np.float32),
        Wq=np.asarray(L0_attn_Wq, np.float32), Wk=np.asarray(L0_attn_Wk, np.float32),
        Wv=np.asarray(L0_attn_Wv, np.float32), Wo=np.asarray(L0_attn_Wo, np.float32),
        gamma=np.asarray(L0_attn_gamma, np.float32), beta=np.asarray(L0_attn_beta, np.float32),
        Wg1=np.asarray(L0_attn_Wg1, np.float32), Wg2=np.asarray(L0_attn_Wg2, np.float32),
        w_gate=np.asarray(L0_ffn_w_gate, np.float32), w_up=np.asarray(L0_ffn_w_up, np.float32),
        w_down=np.asarray(L0_ffn_w_down, np.float32),
    )
    if "nc" not in _NC_CACHE:
        _NC_CACHE["nc"] = build_nc(cfg)
    nc = _NC_CACHE["nc"]
    in_maps = [core_inputs(cfg, i, full) for i in range(cfg.n_cores)]
    res = run_bass_kernel_spmd(nc, in_maps, core_ids=list(range(cfg.n_cores)))
    parts = [np.asarray(res.results[i]["out"]) for i in range(cfg.n_cores)]
    out = np.concatenate(parts, axis=0).reshape(2, 8192, 1024).astype(np.float32)
    return out


# version 39
# speedup vs baseline: 1.1413x; 1.1413x over previous
import contextlib
import numpy as np
import concourse.bass as bass
import concourse.mybir as mybir

F32 = mybir.dt.float32
BF16 = mybir.dt.bfloat16
AX = mybir.AxisListType.X
ALU = mybir.AluOpType
ACTF = mybir.ActivationFunctionType

ENGS = ("sync", "tensor", "vector", "scalar", "gpsimd")


def _ap_region(ap):
    t = ap.tensor
    name = t.name
    shape = list(t.shape)
    space = getattr(t, "space", None)
    is_dram = not hasattr(t, "base_partition")
    if is_dram:
        off = ap.offset
        span = 1
        for step, count in ap.ap:
            if count > 1:
                span += (count - 1) * step
        return (name, "dram", 0, 1, off, off + span)
    rowelems = 1
    for s in shape[1:]:
        rowelems *= s
    off = ap.offset
    plo = off // rowelems
    flo = off % rowelems
    pspan = 1
    fspan = 1
    for step, count in ap.ap:
        if count <= 1:
            continue
        if step >= rowelems and rowelems > 0:
            pspan += (count - 1) * (step // rowelems)
        else:
            fspan += (count - 1) * step
    base = getattr(t, "base_partition", 0) or 0
    if type(t).__name__.startswith("PSum"):
        return (name, "psum", 0, 128, 0, rowelems)
    return (name, "sbuf", base + plo, base + plo + pspan, flo, flo + fspan)


def _overlap(a, b):
    return (a[0] == b[0] and a[2] < b[3] and b[2] < a[3] and a[4] < b[5] and b[4] < a[5])


class Op:
    __slots__ = ("eng", "emit", "kind", "regions_r", "regions_w", "deps",
                 "needs_inc", "inc_val", "dma_sem", "dma_val", "issue_wait",
                 "seq", "emitted_idx")

    def __init__(self, eng, emit, kind):
        self.eng = eng
        self.emit = emit
        self.kind = kind
        self.regions_r = []
        self.regions_w = []
        self.deps = []
        self.needs_inc = False
        self.inc_val = None
        self.dma_sem = None
        self.dma_val = None
        self.issue_wait = None
        self.seq = 0


class Prog:
    def __init__(self, nc, ctx):
        self.nc = nc
        self.ops = []
        self.writers = {}
        self.readers = {}
        self.eng_sem = {e: ctx.enter_context(nc.semaphore(f"es_{e}")) for e in ENGS}
        self.eng_cnt = {e: 0 for e in ENGS}
        self.dma_pools = {}
        for q, n in (("ld", 24), ("g", 8), ("cc", 2), ("fin", 4), ("s", 16)):
            self.dma_pools[q] = {
                "sems": [[ctx.enter_context(nc.semaphore(f"q_{q}{i}")), 0] for i in range(n)],
                "rot": 0}
        self.hw = {}
        self.warm_ap = None
        self.warm_ops = None
        self.phase_idx = 0
        self.phase_bounds = []

    def _add(self, eng, emit, reads, writes, kind="compute", queue="ld", inck=16):
        op = Op(eng, emit, kind)
        deps = []
        for ap in reads:
            r = _ap_region(ap)
            op.regions_r.append(r)
            for (wr, wop) in self.writers.get(r[0], ()):
                if _overlap(r, wr):
                    deps.append(wop)
        for ap in writes:
            r = _ap_region(ap)
            op.regions_w.append(r)
            for (wr, wop) in self.writers.get(r[0], ()):
                if _overlap(r, wr):
                    deps.append(wop)
            for (rr, rop) in self.readers.get(r[0], ()):
                if _overlap(r, rr):
                    deps.append(rop)
        op.deps = deps
        for d in deps:
            if d.kind == "compute" and d.eng != eng:
                d.needs_inc = True
        if kind in ("dma", "cc"):
            pool = self.dma_pools[queue]
            slot = pool["sems"][pool["rot"]]
            pool["rot"] = (pool["rot"] + 1) % len(pool["sems"])
            if slot[1] > 0:
                op.issue_wait = (slot[0], slot[1])
            slot[1] += inck
            op.dma_sem = slot[0]
            op.dma_val = slot[1]
        for r in op.regions_w:
            lst = self.writers.setdefault(r[0], [])
            lst[:] = [(wr, wop) for (wr, wop) in lst
                      if not (wr[2] >= r[2] and wr[3] <= r[3] and wr[4] >= r[4] and wr[5] <= r[5])]
            lst.append((r, op))
            rl = self.readers.get(r[0])
            if rl:
                rl[:] = [(rr, rop) for (rr, rop) in rl if not _overlap(r, rr)]
        for r in op.regions_r:
            self.readers.setdefault(r[0], []).append((r, op))
        self.ops.append(op)
        return op

    def phase(self, name=None):
        self.phase_idx += 1
        for op in self.ops:
            if op.kind == "compute" and op.needs_inc:
                self.eng_cnt[op.eng] += 1
                op.inc_val = self.eng_cnt[op.eng]
            op.seq = 1
        per_eng = {e: [] for e in ENGS}
        for op in self.ops:
            per_eng[op.eng].append(op)
        prog = self

        def make_body(eng, ops):
            def body(e):
                drained_upto = [0]
                emitted = [0]
                for op in ops:
                    need_drain = False
                    if op.issue_wait is not None:
                        h, v = op.issue_wait
                        key = (eng, id(h))
                        if prog.hw.get(key, -1) < v:
                            e.wait_ge(h, v)
                            prog.hw[key] = v
                    for d in op.deps:
                        if d.kind in ("dma", "cc"):
                            key = (eng, id(d.dma_sem))
                            if prog.hw.get(key, -1) < d.dma_val:
                                e.wait_ge(d.dma_sem, d.dma_val)
                                prog.hw[key] = d.dma_val
                        elif d.eng == eng:
                            if eng in ("vector", "scalar", "gpsimd") and d.emitted_idx >= drained_upto[0]:
                                need_drain = True
                        else:
                            key = (eng, id(prog.eng_sem[d.eng]))
                            if d.inc_val is not None and prog.hw.get(key, -1) < d.inc_val:
                                e.wait_ge(prog.eng_sem[d.eng], d.inc_val)
                                prog.hw[key] = d.inc_val
                    if need_drain:
                        e.drain()
                        drained_upto[0] = emitted[0]
                    inc = None
                    if op.kind == "compute" and op.inc_val is not None:
                        inc = (prog.eng_sem[eng], 1)
                    elif op.kind == "dma":
                        inc = (op.dma_sem, 16)
                    elif op.kind == "cc":
                        inc = (op.dma_sem, 1)
                    if not hasattr(prog, "imap"):
                        prog.imap = []
                    prog.imap.append(
                        (prog.nc.n_instructions(), eng, op.kind,
                         [r[0] for r in op.regions_w], [r[0] for r in op.regions_r]))
                    ret = op.emit(e)
                    if inc is not None and ret is not None:
                        ret.then_inc(*inc)
                    op.emitted_idx = emitted[0]
                    emitted[0] += 1
            return body

        idx = {e: 0 for e in ENGS}
        for op in self.ops:
            op.emitted_idx = idx[op.eng]
            idx[op.eng] += 1

        import contextlib as _ctxlib
        _i0 = self.nc.n_instructions()
        scope = self.nc.named_scope(name) if name else _ctxlib.nullcontext()
        with scope:
            with self.nc.Block(name) as block:
                for eng in ENGS:
                    if per_eng[eng]:
                        getattr(block, eng)(make_body(eng, per_eng[eng]))
        self.phase_bounds.append((name, _i0, self.nc.n_instructions()))
        for d in (self.writers, self.readers):
            for k in list(d.keys()):
                d[k] = [(r, op) for (r, op) in d[k] if op.kind in ("dma", "cc")]
        self.ops = []

    def ld(self, out, in_, queue="ld"):
        self._add("sync", lambda e, o=out, i=in_: e.dma_start(out=o, in_=i),
                  [in_], [out], kind="dma", queue=queue)

    def st(self, out, in_, queue="ld"):
        self._add("sync", lambda e, o=out, i=in_: e.dma_start(out=o, in_=i),
                  [in_], [out], kind="dma", queue=queue)

    def gld(self, out, in_, queue="g"):
        self._add("gpsimd", lambda e, o=out, i=in_: e.dma_start(out=o, in_=i),
                  [in_], [out], kind="dma", queue=queue)

    def sld(self, out, in_, queue="s"):
        self._add("scalar", lambda e, o=out, i=in_: e.dma_start(out=o, in_=i),
                  [in_], [out], kind="dma", queue=queue)

    def cc(self, kind, alu, groups, in_ap, out_ap):
        self._add("gpsimd", lambda e: e.collective_compute(
            kind, alu, replica_groups=groups, ins=[in_ap], outs=[out_ap]),
            [in_ap], [out_ap], kind="cc", queue="cc", inck=1)

    def mm(self, out, lhsT, rhs, start=True, stop=True):
        self._add("tensor", lambda e, o=out, l=lhsT, r=rhs, s1=start, s2=stop:
                  e.matmul(o, l, r, start=s1, stop=s2), [lhsT, rhs], [out])

    def mmg(self, out, pairs):
        n = len(pairs)
        def emit(e, o=out, pairs=tuple(pairs), n=n):
            ret = None
            for i, (l, r) in enumerate(pairs):
                ret = e.matmul(o, l, r, start=(i == 0), stop=(i == n - 1))
            return ret
        reads = [out]
        for l, r in pairs:
            reads += [l, r]
        self._add("tensor", emit, reads, [out])

    def tr(self, out, in_, ident):
        self._add("tensor", lambda e, o=out, i=in_, d=ident: e.transpose(o, i, d),
                  [in_, ident], [out])

    def cp(self, eng, out, in_):
        self._add(eng, lambda e, o=out, i=in_: e.tensor_copy(out=o, in_=i), [in_], [out])

    def ts(self, eng, out, in0, s1, op0, s2=None, op1=None):
        reads = [in0] + [s for s in (s1, s2) if isinstance(s, bass.AP)]
        self._add(eng, lambda e, o=out, i=in0, s1=s1, s2=s2, op0=op0, op1=op1:
                  e.tensor_scalar(out=o, in0=i, scalar1=s1, scalar2=s2, op0=op0, **({"op1": op1} if op1 is not None else {})),
                  reads, [out])

    def tt(self, eng, out, in0, in1, op):
        self._add(eng, lambda e, o=out, a=in0, b=in1, op=op: e.tensor_tensor(out=o, in0=a, in1=b, op=op),
                  [in0, in1], [out])

    def stt(self, eng, out, in0, scalar, in1, op0, op1):
        reads = [in0, in1] + ([scalar] if isinstance(scalar, bass.AP) else [])
        self._add(eng, lambda e, o=out, a=in0, s=scalar, b=in1, op0=op0, op1=op1:
                  e.scalar_tensor_tensor(out=o, in0=a, scalar=s, in1=b, op0=op0, op1=op1),
                  reads, [out])

    def red(self, eng, out, in_, op, absval=False):
        self._add(eng, lambda e, o=out, i=in_, op=op, a=absval:
                  e.tensor_reduce(out=o, in_=i, axis=AX, op=op, apply_absolute_value=a),
                  [in_], [out])

    def act(self, out, in_, func, bias=0.0, scale=1.0, accum=None):
        reads = [in_] + [s for s in (bias, scale) if isinstance(s, bass.AP)]
        writes = [out] + ([accum] if accum is not None else [])
        if isinstance(bias, float):
            bias = np.float32(bias)
        if isinstance(scale, float):
            scale = np.float32(scale)
        self._add("scalar", lambda e, o=out, i=in_, f=func, b=bias, s=scale, a=accum:
                  e.activation(out=o, in_=i, func=f, bias=b, scale=s, accum_out=a),
                  reads, writes)

    def recip(self, out, in_):
        self._add("vector", lambda e, o=out, i=in_: e.reciprocal(out=o, in_=i), [in_], [out])

    def mset(self, eng, ap, v):
        self._add(eng, lambda e, a=ap, v=v: e.memset(a, v), [], [ap])

    def iota(self, ap, pattern, mult):
        self._add("gpsimd", lambda e, a=ap, p=pattern, m=mult: e.iota(a, p, channel_multiplier=m,
                  allow_small_or_imprecise_dtypes=True), [], [ap])

    def raw(self, eng, emit, reads, writes):
        self._add(eng, emit, reads, writes)

    def warm(self):
        if self.warm_ap is None:
            return
        o, l, r = self.warm_ap, self.warm_ops[0], self.warm_ops[1]
        self._add("tensor", lambda e, o=o, l=l, r=r: e.matmul(o, l, r, start=True, stop=True),
                  [l, r], [o])

    def dma_fence(self):
        self.wait_all_dma("sync")

    def dma_fence_nocc(self):
        for q, pool in self.dma_pools.items():
            if q == "cc":
                continue
            for (h, v) in pool["sems"]:
                if v > 0:
                    self._add("sync", lambda e, h=h, v=v: e.wait_ge(h, v), [], [])

    def wait_all_dma(self, eng):
        for q, pool in self.dma_pools.items():
            for (h, v) in pool["sems"]:
                if v > 0:
                    self._add(eng, lambda e, h=h, v=v: e.wait_ge(h, v), [], [])


import contextlib


M1 = float(np.float32(1.5 * 2.0**23))
M2 = float(np.float32(1.5 * 2.0**52))
DM = float(1.5 * 2.0**52 - 1.5 * 2.0**23)


class CFG:
    def __init__(self, Tc=2048, n_cores=8, group=4, sup=4):
        self.E = 1024
        self.H = 16
        self.Dh = 64
        self.INNER = 4096
        self.CH = 128
        self.Tc = Tc
        self.n_st = Tc // 128
        self.sup = sup
        self.n_sup = self.n_st // sup
        self.n_cores = n_cores
        self.group = group
        assert self.n_st % sup == 0 and n_cores % group == 0


def _build_slopes(n):
    def po2(m):
        start = 2.0 ** (-2.0 ** (-(np.log2(m) - 3)))
        return [start * start**i for i in range(m)]
    if np.log2(n).is_integer():
        return po2(n)
    closest = 2 ** int(np.floor(np.log2(n)))
    return po2(closest) + _build_slopes(2 * closest)[0::2][: n - closest]


def host_tables(cfg):
    H, CH, n_st = cfg.H, cfg.CH, cfg.n_st
    s32 = np.array(_build_slopes(H), dtype=np.float32)
    s32 = s32 * (1.0 - 0.0 / 11.0 + 1e-05)
    s = s32.astype(np.float64)
    idx = np.arange(CH, dtype=np.float64)
    diff = idx[:, None] - idx[None, :]
    mask = np.where(diff >= 0, np.exp(-s[:, None, None] * diff), 0.0)
    maskT = np.ascontiguousarray(mask.transpose(0, 2, 1)).astype(np.float32)
    lam_q = np.exp(-s[:, None] * (idx + 1.0))
    lam_k = np.exp(-s[:, None] * (CH - 1.0 - idx))
    lam_c = np.exp(-s * CH)
    gam = lam_q[:, :, None] * (lam_c[:, None, None] ** np.arange(n_st)[None, None, :])
    Dh = 64
    lamk_full = np.repeat(lam_k[:, None, :], Dh, axis=1)
    lamk_full = lamk_full.transpose(2, 0, 1).reshape(CH, H * Dh)
    lamc_full = np.broadcast_to(np.repeat(lam_c, Dh).reshape(1, H * Dh), (Dh, H * Dh))
    gf = gam.transpose(2, 1, 0)
    gam_full = np.repeat(gf[:, :, :, None], Dh, axis=3).reshape(n_st * CH, H * Dh)
    return dict(
        maskT=maskT,
        lamq=np.ascontiguousarray(lam_q.T).astype(np.float32),
        lamc=[float(v) for v in lam_c],
        lamk_full=np.ascontiguousarray(lamk_full).astype(np.float32),
        lamc_full=np.ascontiguousarray(lamc_full).astype(np.float32),
        gam_full=np.ascontiguousarray(gam_full).astype(np.float32),
    )


def host_cvec(cfg, pos):
    H, n_st = cfg.H, cfg.n_st
    s32 = np.array(_build_slopes(H), dtype=np.float32)
    s32 = s32 * (1.0 - 0.0 / 11.0 + 1e-05)
    lam_c = np.exp(-s32.astype(np.float64) * cfg.CH)
    cv = np.zeros((cfg.group, H), dtype=np.float64)
    for j in range(pos):
        cv[j] = lam_c ** (n_st * (pos - 1 - j))
    out = np.broadcast_to(cv.reshape(1, cfg.group * H), (64, cfg.group * H))
    return np.ascontiguousarray(out).astype(np.float32)


def build_nc(cfg, dbg=False):
    nc = bass.Bass("TRN2", target_bir_lowering=False)
    E, H, Dh, INNER, CH = cfg.E, cfg.H, cfg.Dh, cfg.INNER, cfg.CH
    Tc, n_st, SUP, n_sup = cfg.Tc, cfg.n_st, cfg.sup, cfg.n_sup
    NOC = E // 128
    NO5 = INNER // 512
    NOCI = INNER // 128
    SC = min(512, Tc)
    NS5 = Tc // SC
    lamc = host_tables(cfg)["lamc"]

    x0_d = nc.dram_tensor("x0", [Tc, E], F32, kind="ExternalInput")
    wq_d = nc.dram_tensor("wq", [E, E], F32, kind="ExternalInput")
    wk_d = nc.dram_tensor("wk", [E, E], F32, kind="ExternalInput")
    wv_d = nc.dram_tensor("wv", [E, E], F32, kind="ExternalInput")
    wo_d = nc.dram_tensor("wo", [E, E], F32, kind="ExternalInput")
    wg1_d = nc.dram_tensor("wg1", [E, Dh], F32, kind="ExternalInput")
    wg2_d = nc.dram_tensor("wg2", [Dh, E], F32, kind="ExternalInput")
    wgt_d = nc.dram_tensor("wgt", [E, INNER], F32, kind="ExternalInput")
    wut_d = nc.dram_tensor("wut", [E, INNER], F32, kind="ExternalInput")
    wdt_d = nc.dram_tensor("wdt", [INNER, E], F32, kind="ExternalInput")
    gab_d = nc.dram_tensor("gab", [128, 2 * E], F32, kind="ExternalInput")
    maskt_d = nc.dram_tensor("maskt", [H, CH, CH], F32, kind="ExternalInput")
    lamq_d = nc.dram_tensor("lamq", [CH, H], F32, kind="ExternalInput")
    lamkf_d = nc.dram_tensor("lamkf", [CH, H * Dh], F32, kind="ExternalInput")
    lamcf_d = nc.dram_tensor("lamcf", [64, H * Dh], F32, kind="ExternalInput")
    gamf_d = nc.dram_tensor("gamf", [n_st * CH, H * Dh], F32, kind="ExternalInput")
    cvec_d = nc.dram_tensor("cvec", [64, cfg.group * H], F32, kind="ExternalInput")

    wgq_d = nc.dram_tensor("wgq", [INNER // 512, 128, (E // 128) * 512], BF16)
    wuq_d = nc.dram_tensor("wuq", [INNER // 512, 128, (E // 128) * 512], BF16)
    wdq_d = nc.dram_tensor("wdq", [INNER // 256, 128, 2 * E], BF16)
    qsp_d = nc.dram_tensor("qsp", [Tc // 128, 64, H * 128], F32)
    ksp_d = nc.dram_tensor("ksp", [Tc // 128, 64, H * 128], F32)
    vsp_d = nc.dram_tensor("vsp", [Tc, E], F32)
    osp_d = nc.dram_tensor("osp", [Tc, E], F32)
    cin_d = nc.dram_tensor("cin", [64, H * Dh], F32)
    bnc_d = nc.dram_tensor("bnc", [1, 12], F32)
    cout_d = nc.dram_tensor("cout", [cfg.group * 64, H * Dh], F32)
    out_d = nc.dram_tensor("out", [Tc, E], F32, kind="ExternalOutput")
    if dbg:
        dkv_d = nc.dram_tensor("dkv", [64, H * Dh], F32, kind="ExternalOutput")
        doc_d = nc.dram_tensor("doc", [Tc, E], F32, kind="ExternalOutput")

    ctx = contextlib.ExitStack()
    with ctx:
        xr = ctx.enter_context(nc.sbuf_tensor("xr", [128, n_st, E], F32))
        idf = ctx.enter_context(nc.sbuf_tensor("idf", [128, 128], F32))
        idb = ctx.enter_context(nc.sbuf_tensor("idb", [128, 128], BF16))
        ind = ctx.enter_context(nc.sbuf_tensor("ind", [128, 1], F32))
        omg = ctx.enter_context(nc.sbuf_tensor("omg", [128, 1], F32))
        mb_ = ctx.enter_context(nc.sbuf_tensor("mb_", [128, 1], F32))
        thr = ctx.enter_context(nc.sbuf_tensor("thr", [128, 12], F32))
        st1 = ctx.enter_context(nc.sbuf_tensor("st1", [128, 16], F32))
        ones = ctx.enter_context(nc.sbuf_tensor("ones", [128, 4], F32))
        oneK = ctx.enter_context(nc.sbuf_tensor("oneK", [1, 128], F32))

        pA = ctx.enter_context(nc.psum_tensor("pA", [128, 512], F32))
        pB = ctx.enter_context(nc.psum_tensor("pB", [128, 512], F32))
        pC = ctx.enter_context(nc.psum_tensor("pC", [128, 512], F32))
        pD = ctx.enter_context(nc.psum_tensor("pD", [128, 512], F32))
        pE_ = ctx.enter_context(nc.psum_tensor("pE", [128, 512], F32))
        pF = ctx.enter_context(nc.psum_tensor("pF", [128, 512], F32))
        banks = [pA, pB, pC, pD, pE_, pF]

        P = Prog(nc, ctx)
        globals()["_LAST_P"] = P
        P.warm_ap = pE_[0:64, 0:128]
        P.warm_ops = (idb[:, 0:64], idb[:])

        with nc.sbuf_tensor("pidx", [128, 128], F32) as pidx:
            for t in range(n_st):
                P.ld(xr[:, t, :], x0_d.ap()[t * 128:(t + 1) * 128, :])
            P.iota(bass.AP(idf, 0, [[128, 128], [1, 128]]), [[1, 128]], 0)
            P.iota(bass.AP(pidx, 0, [[128, 128], [1, 128]]), [[0, 128]], 1)
            P.tt("vector", idf[:], idf[:], pidx[:, 0:128], ALU.is_equal)
            P.cp("vector", idb[:], idf[:])
            P.mset("vector", ind[:], 0.75)
            P.ts("vector", ind[:], ind[:], M2, ALU.add)
            P.ts("vector", ind[:], ind[:], -M2, ALU.add)
            P.ts("vector", omg[:], ind[:], -1.0, ALU.mult, 1.0, ALU.add)
            P.ts("vector", mb_[:], omg[:], -M1, ALU.mult)
            P.mset("vector", ones[:], 1.0)
            P.mset("vector", oneK[:], 1.0)
            P.phase("setup")

        with nc.sbuf_tensor("wsg", [128, 2, 4096], F32) as wsg, \
             nc.sbuf_tensor("acc3", [128, 8], F32) as acc3, \
             nc.sbuf_tensor("prt", [128, 4], F32) as prt, \
             nc.sbuf_tensor("thsrc", [1, 12], F32) as thsrc:
            P.mset("vector", acc3[:], 0.0)
            tiles = []
            for m, wd in enumerate((wgt_d, wut_d, wdt_d)):
                rows, cols = wd.shape
                if cols == 4096:
                    for r0 in range(0, rows, 128):
                        tiles.append((m, wd.ap()[r0:r0 + 128, :], 1))
                else:
                    for r0 in range(0, rows, 512):
                        tiles.append((m, wd.ap()[r0:r0 + 512, :].rearrange(
                            "(c p) o -> p c o", p=128), 4))
            for ti, (m, lap, rch) in enumerate(tiles):
                sl = ti % 2
                dst = wsg[:, sl, :] if rch == 1 else \
                    wsg[:, sl, :].rearrange("p (c o) -> p c o", c=rch)
                (P.ld if sl == 0 else P.sld)(dst, lap)
                if sl == 0:
                    P.act(wsg[:, sl, 0:4096], wsg[:, sl, 0:4096], ACTF.Abs,
                          accum=prt[:, 0:1])
                    P.tt("vector", acc3[:, m:m + 1], acc3[:, m:m + 1], prt[:, 0:1], ALU.add)
                else:
                    P.red("vector", prt[:, 1:2], wsg[:, sl, 0:4096], ALU.add, absval=True)
                    P.tt("vector", acc3[:, 4 + m:5 + m], acc3[:, 4 + m:5 + m], prt[:, 1:2], ALU.add)
            P.tt("vector", acc3[:, 0:3], acc3[:, 0:3], acc3[:, 4:7], ALU.add)
            P.mm(pA[0:4, 0:1], acc3[:, 0:4], ones[:, 0:1])
            nel = float(E * INNER)
            P.ts("vector", st1[0:4, 0:1], pA[0:4, 0:1], 1.0 / nel, ALU.mult, 1e-5, ALU.max)
            P.ts("vector", st1[0:4, 1:2], st1[0:4, 0:1], 0.5, ALU.mult)
            P.ts("vector", st1[0:4, 2:3], st1[0:4, 1:2], -1.0, ALU.mult)
            P.st(bnc_d.ap()[0:1, 0:4], st1[0:4, 0:1])
            P.st(bnc_d.ap()[0:1, 4:8], st1[0:4, 1:2])
            P.st(bnc_d.ap()[0:1, 8:12], st1[0:4, 2:3])
            P.ld(thsrc[0:1, 0:12], bnc_d.ap())
            P.mm(pA[:, 0:12], oneK[:], thsrc[:])
            P.cp("vector", thr[:, 0:3], pA[:, 0:3])
            P.cp("vector", thr[:, 3:6], pA[:, 4:7])
            P.cp("vector", thr[:, 6:9], pA[:, 8:11])
            P.phase("wscale")

        with nc.sbuf_tensor("qw_s", [128, 2, 4096], F32) as qw_s, \
             nc.sbuf_tensor("qo", [128, 2, 4096], BF16) as qo, \
             nc.sbuf_tensor("qt1", [128, 4096], BF16) as qt1, \
             nc.sbuf_tensor("qt2", [128, 4096], BF16) as qt2, \
             nc.sbuf_tensor("qt3", [128, 4096], BF16) as qt3, \
             nc.sbuf_tensor("qt4", [128, 4096], BF16) as qt4:
            tiles = []
            for m2, (wd, scr, thix) in enumerate(((wgt_d, wgq_d, 3), (wut_d, wuq_d, 4))):
                for c in range(E // 128):
                    tiles.append((wd.ap()[c * 128:(c + 1) * 128, :],
                                  scr.ap().rearrange("n p o -> p n o")[:, :, c * 512:(c + 1) * 512],
                                  thix, 1))
            for t in range(INNER // 512):
                tiles.append((wdt_d.ap()[t * 512:(t + 1) * 512, :].rearrange(
                    "(c p) o -> p c o", p=128),
                    wdq_d.ap()[2 * t:2 * t + 2].rearrange("g p o -> p g o"),
                    5, 4))
            for ti, (lap, sap, thix, rch) in enumerate(tiles):
                sl = ti % 2
                dst = qw_s[:, sl, :] if rch == 1 else \
                    qw_s[:, sl, :].rearrange("p (c o) -> p c o", c=rch)
                P.ld(dst, lap)
                w_ = qw_s[:, sl, 0:4096]
                o_ = qo[:, sl, 0:4096]
                if ti % 2 == 0:
                    P.ts("vector", qt1[:, :], w_, thr[:, thix:thix + 1], ALU.is_ge)
                    P.ts("vector", qt2[:, :], w_, -1.0, ALU.mult, thr[:, thix:thix + 1], ALU.is_ge)
                    P.tt("vector", o_, qt1[:, :], qt2[:, :], ALU.subtract)
                else:
                    P.act(qt3[:, :], w_, ACTF.Relu, bias=thr[:, thix + 3:thix + 4])
                    P.act(qt4[:, :], w_, ACTF.Relu, bias=thr[:, thix + 3:thix + 4], scale=-1.0)
                    P.tt("vector", qt3[:, :], qt3[:, :], qt4[:, :], ALU.subtract)
                    P.act(o_, qt3[:, :], ACTF.Sign)
                P.gld(sap, o_)
            P.dma_fence()
            P.phase("wquant")

        def _sl2(ap3, lo, hi):
            return ap3[:, lo:hi] if len(ap3.shape) == 2 else ap3[:, :, lo:hi]

        def quant_round(dst, src, scale_ap, t1, t2, width=None, center=None):
            n = width if width is not None else src.shape[-1]
            for i, lo in enumerate(range(0, n, 512)):
                hi = min(lo + 512, n)
                w = hi - lo
                q = i % t1.shape[1]
                q2 = i % t2.shape[1]
                s_ = _sl2(src, lo, hi)
                d_ = _sl2(dst, lo, hi)
                P.warm()
                if center is not None:
                    P.ts("vector", t2[:, q2, 0:w], s_, scale_ap, ALU.mult, center, ALU.add)
                    P.act(t1[:, q, 0:w], t2[:, q2, 0:w], ACTF.Identity, bias=M1)
                else:
                    P.act(t1[:, q, 0:w], s_, ACTF.Identity, bias=M1, scale=scale_ap)
                P.ts("vector", t2[:, q2, 0:w], t1[:, q, 0:w], DM, ALU.add)
                P.act(t1[:, q, 0:w], t1[:, q, 0:w], ACTF.Identity, bias=mb_[:], scale=omg[:])
                P.stt("vector", d_, t2[:, q2, 0:w], -M2, t1[:, q, 0:w], ALU.add, ALU.add)

        def ternarize(dst, src, thr_ap, t1, t2):
            n = src.shape[-1]
            dn = dst.shape[-1]
            for lo in range(0, n, 512):
                hi = min(lo + 512, n)
                w = hi - lo
                s_ = _sl2(src, lo, hi)
                if dn == n or len(dst.shape) == 2:
                    d_ = _sl2(dst, lo, hi)
                else:
                    o = dst.shape[-1]
                    assert lo % o == 0 and hi % o == 0
                    d_ = dst[:, lo // o:hi // o, :]
                q = (lo // 512) % 2
                P.warm()
                P.ts("vector", t1[:, q, 0:w], s_, thr_ap, ALU.is_ge)
                P.ts("vector", t2[:, q, 0:w], s_, -1.0, ALU.mult)
                P.ts("vector", t2[:, q, 0:w], t2[:, q, 0:w], thr_ap, ALU.is_ge)
                P.tt("vector", d_, t1[:, q, 0:w], t2[:, q, 0:w], ALU.subtract)

        def emit_ffn(first):
            fx = "a" if first else "b"
            zz = contextlib.ExitStack()
            with zz:
                _saved_warm = P.warm_ap
                P.warm_ap = None
                zz.callback(lambda: setattr(P, "warm_ap", _saved_warm))
                gu = zz.enter_context(nc.sbuf_tensor(f"gu{fx}", [128, SUP, INNER], F32))
                xqt = zz.enter_context(nc.sbuf_tensor(f"xqt{fx}", [128, 2, NOC, SUP * 128], BF16))
                xq2 = zz.enter_context(nc.sbuf_tensor(f"xq2{fx}", [128, 2, 1024], BF16))
                hqt = zz.enter_context(nc.sbuf_tensor(f"hqt{fx}", [128, 2, NOCI * 128], BF16))
                wgr = zz.enter_context(nc.sbuf_tensor(f"wgr{fx}", [128, 2, NOC, 512], BF16))
                wur = zz.enter_context(nc.sbuf_tensor(f"wur{fx}", [128, 1, NOC, 512], BF16))
                wdr = zz.enter_context(nc.sbuf_tensor(f"wdr{fx}", [128, 3, 2, E], BF16))
                tq1 = zz.enter_context(nc.sbuf_tensor(f"tq1{fx}", [128, 2, 512], F32))
                tq2 = zz.enter_context(nc.sbuf_tensor(f"tq2{fx}", [128, 1, 512], F32))
                fst = zz.enter_context(nc.sbuf_tensor(f"fst{fx}", [128, 2, SUP, 8], F32))
                pTa_g = nc.psum_tensor(f"pTa{fx}", [128, 1024], BF16)
                pTb_g = nc.psum_tensor(f"pTb{fx}", [128, 1024], BF16)
                pT2_ = [pTa_g.__enter__(), pTb_g.__enter__()]

                def stepA_chain(sp, u):
                    if True:
                        fsl = sp % 2
                        t = sp * SUP + u
                        xt = xr[:, t, :]
                        P.red("vector", st1[:, 0:1], xt, ALU.add)
                        P.ts("vector", fst[:, fsl, u, 0:1], st1[:, 0:1], -1.0 / E, ALU.mult)
                        P.warm()
                        P.act(xq2[:, u % 2, 0:E], xt, ACTF.Square,
                              bias=fst[:, fsl, u, 0:1], accum=st1[:, 1:2])
                        P.act(st1[:, 2:3], st1[:, 1:2], ACTF.Sqrt, bias=1e-5, scale=1.0 / E)
                        P.recip(fst[:, fsl, u, 1:2], st1[:, 2:3])
                        P.red("vector", st1[:, 3:4], xt, ALU.max)
                        P.red("vector", st1[:, 13:14], xt, ALU.min)
                        P.tt("vector", st1[:, 3:4], st1[:, 3:4], fst[:, fsl, u, 0:1], ALU.add)
                        P.tt("vector", st1[:, 13:14], st1[:, 13:14], fst[:, fsl, u, 0:1], ALU.add)
                        P.ts("vector", st1[:, 13:14], st1[:, 13:14], -1.0, ALU.mult)
                        P.tt("vector", st1[:, 3:4], st1[:, 3:4], st1[:, 13:14], ALU.max)
                        P.warm()
                        P.tt("vector", st1[:, 4:5], st1[:, 3:4], fst[:, fsl, u, 1:2], ALU.mult)
                        P.ts("vector", st1[:, 5:6], st1[:, 4:5], 1e-5, ALU.max)
                        P.recip(st1[:, 6:7], st1[:, 5:6])
                        P.ts("vector", st1[:, 7:8], st1[:, 6:7], 127.0, ALU.mult)
                        P.ts("vector", fst[:, fsl, u, 3:4], st1[:, 5:6], thr[:, 0:1], ALU.mult, 1.0 / 127.0, ALU.mult)
                        P.ts("vector", fst[:, fsl, u, 4:5], st1[:, 5:6], thr[:, 1:2], ALU.mult, 1.0 / 127.0, ALU.mult)
                        P.tt("vector", st1[:, 8:9], fst[:, fsl, u, 1:2], st1[:, 7:8], ALU.mult)
                        P.tt("vector", st1[:, 14:15], fst[:, fsl, u, 0:1], st1[:, 8:9],
                             ALU.mult)
                        quant_round(xq2[:, u % 2, 0:E], xt, st1[:, 8:9], tq1, tq2,
                                    center=st1[:, 14:15])

                def stepA_trans(sp, u):
                    fsl = sp % 2
                    pt = pT2_[u % 2]
                    for ich in range(NOC):
                        P.tr(pt[:, ich * 128:(ich + 1) * 128],
                             xq2[:, u % 2, ich * 128:(ich + 1) * 128], idb[:])
                    P.cp("vector", xqt[:, sp % 2, 0:NOC, u * 128:(u + 1) * 128], pt[:, 0:1024])
                    P.mset("vector", fst[:, fsl, u, 5:6], 0.0)

                def stepA_stile(sp, u):
                    stepA_chain(sp, u)
                    stepA_trans(sp, u)

                def stepA(sp):
                    for u in range(SUP):
                        stepA_stile(sp, u)

                def stepB(sp, nxt=None):
                    fsl = sp % 2
                    xs = sp % 2
                    for oc in range(NO5):
                        if nxt and oc % 2 == 1 and (oc // 2) < SUP:
                            u_ = oc // 2
                            if u_ > 0:
                                stepA_trans(sp + 1, u_ - 1)
                            stepA_chain(sp + 1, u_)
                        wv_ = oc % 2
                        P.ld(wgr[:, wv_, :, :], wgq_d.ap()[oc])
                        P.gld(wur[:, 0, :, :], wuq_d.ap()[oc])
                        for u in range(SUP):
                            P.mmg(banks[u][:, 0:512], [(xqt[:, xs, i, u * 128:(u + 1) * 128], wur[:, 0, i, :]) for i in range(NOC)])
                        for u in range(SUP):
                            pg = banks[4 + (u % 2)]
                            P.mmg(pg[:, 0:512], [(xqt[:, xs, i, u * 128:(u + 1) * 128], wgr[:, wv_, i, :]) for i in range(NOC)])
                            guc = gu[:, u, oc * 512:(oc + 1) * 512]
                            P.act(guc, pg[:, 0:512], ACTF.Silu, scale=fst[:, fsl, u, 3:4])
                            P.stt("vector", guc, banks[u][:, 0:512], fst[:, fsl, u, 4:5], guc, ALU.mult, ALU.mult)
                            P.red("vector", st1[:, 9:10], guc, ALU.max, absval=True)
                            P.tt("vector", fst[:, fsl, u, 5:6], fst[:, fsl, u, 5:6], st1[:, 9:10], ALU.max)

                def c_group(sp, up, ui):
                    fsl = sp % 2
                    u = up * 2 + ui
                    P.ts("vector", st1[:, 10:11], fst[:, fsl, u, 5:6], 1e-5, ALU.max)
                    P.recip(st1[:, 11:12], st1[:, 10:11])
                    P.ts("vector", fst[:, fsl, u, 6:7], st1[:, 11:12], 127.0, ALU.mult)
                    P.ts("vector", fst[:, fsl, u, 7:8], st1[:, 10:11], thr[:, 2:3], ALU.mult, 0.5 / 127.0, ALU.mult)
                    for sl in range(INNER // 1024):
                        quant_round(xq2[:, sl % 2, 0:1024],
                                    gu[:, u, sl * 1024:(sl + 1) * 1024],
                                    fst[:, fsl, u, 6:7], tq1, tq2)
                        pt = pT2_[sl % 2]
                        for k in range(8):
                            P.tr(pt[:, k * 128:(k + 1) * 128],
                                 xq2[:, sl % 2, k * 128:(k + 1) * 128], idb[:])
                        P.cp("vector", hqt[:, ui, sl * 1024:(sl + 1) * 1024], pt[:, 0:1024])

                def stepC(sp, last, nxt):
                    fsl = sp % 2
                    if nxt:
                        stepA_trans(sp + 1, SUP - 1)
                    for up in range(SUP // 2):
                        c_group(sp, up, 0)
                        c_group(sp, up, 1)
                        for g2 in range(NOCI // 2):
                            slot = g2 % 3
                            P.ld(wdr[:, slot, :, :], wdq_d.ap()[g2])
                            for j in range(2):
                                och = g2 * 2 + j
                                for ui in range(2):
                                    for eh in range(2):
                                        P.mm(banks[2 * ui + eh][:, 0:512],
                                             hqt[:, ui, och * 128:(och + 1) * 128],
                                             wdr[:, slot, j, eh * 512:(eh + 1) * 512],
                                             start=(och == 0), stop=(och == NOCI - 1))
                        for ui in range(2):
                            u = up * 2 + ui
                            t = sp * SUP + u
                            P.stt("vector", xr[:, t, 0:512], banks[2 * ui][:, 0:512], fst[:, fsl, u, 7:8],
                                  xr[:, t, 0:512], ALU.mult, ALU.add)
                            P.stt("vector", xr[:, t, 512:1024], banks[2 * ui + 1][:, 0:512], fst[:, fsl, u, 7:8],
                                  xr[:, t, 512:1024], ALU.mult, ALU.add)
                    if last:
                        P.phase(f"ffn{fx}")

                stepA(0)
                for sp in range(n_sup):
                    nx = sp + 1 < n_sup
                    stepB(sp, nxt=nx)
                    stepC(sp, last=(sp == n_sup - 1), nxt=nx)
                pTb_g.__exit__(None, None, None)
                pTa_g.__exit__(None, None, None)

        emit_ffn(first=True)

        za = contextlib.ExitStack()
        with za:
            lamq_s = za.enter_context(nc.sbuf_tensor("lamq_s", [CH, H], F32))
            kv_a = za.enter_context(nc.sbuf_tensor("kv_a", [64, H * Dh], F32))
            g1f = za.enter_context(nc.sbuf_tensor("g1f", [64, Tc], F32))
            g1h = za.enter_context(nc.sbuf_tensor("g1h", [64, Tc], BF16))
            g1l = za.enter_context(nc.sbuf_tensor("g1l", [64, Tc], BF16))
            kv_b = za.enter_context(nc.sbuf_tensor("kv_b", [64, H * Dh], F32))
            cv_s = za.enter_context(nc.sbuf_tensor("cv_s", [64, cfg.group * H], F32))

            zb3 = contextlib.ExitStack()
            with zb3:
                h2h = zb3.enter_context(nc.sbuf_tensor("h2h", [128, NOC, Tc], BF16))
                h2l = zb3.enter_context(nc.sbuf_tensor("h2l", [128, NOC, Tc], BF16))
                tmh = zb3.enter_context(nc.sbuf_tensor("tmh", [128, E], BF16))
                tml = zb3.enter_context(nc.sbuf_tensor("tml", [128, E], BF16))
                dq2 = zb3.enter_context(nc.sbuf_tensor("dq2", [128, E], F32))
                pTh = zb3.enter_context(nc.psum_tensor("pTh", [128, 1024], BF16))
                pTl = zb3.enter_context(nc.psum_tensor("pTl", [128, 1024], BF16))
                P.ld(lamq_s.ap(), lamq_d.ap())
                for t in range(n_st):
                    xt = xr[:, t, :]
                    P.warm()
                    P.act(dq2[:], xt, ACTF.Identity, accum=st1[:, 0:1])
                    P.ts("vector", st1[:, 1:2], st1[:, 0:1], 1.0 / E, ALU.mult)
                    P.ts("vector", dq2[:], xt, st1[:, 1:2], ALU.subtract)
                    P.act(tmh.ap(), dq2[:], ACTF.Square, accum=st1[:, 2:3])
                    P.act(st1[:, 3:4], st1[:, 2:3], ACTF.Sqrt, bias=1e-5, scale=1.0 / E)
                    P.recip(st1[:, 4:5], st1[:, 3:4])
                    P.ts("vector", dq2[:], dq2[:], st1[:, 4:5], ALU.mult)
                    P.cp("vector", tmh[:], dq2[:])
                    P.tt("vector", tml[:], dq2[:], tmh[:], ALU.subtract)
                    for ich in range(NOC):
                        P.tr(pTh[:, ich * 128:(ich + 1) * 128], tmh[:, ich * 128:(ich + 1) * 128], idb[:])
                        P.tr(pTl[:, ich * 128:(ich + 1) * 128], tml[:, ich * 128:(ich + 1) * 128], idb[:])
                    P.cp("vector", h2h[:, 0:NOC, t * 128:(t + 1) * 128], pTh[:, 0:1024])
                    P.cp("vector", h2l[:, 0:NOC, t * 128:(t + 1) * 128], pTl[:, 0:1024])

                zw = contextlib.ExitStack()
                with zw:
                    wkf = zw.enter_context(nc.sbuf_tensor("wkf", [128, 2, E], F32))
                    wkh = zw.enter_context(nc.sbuf_tensor("wkh", [128, NOC, E], BF16))
                    wkl = zw.enter_context(nc.sbuf_tensor("wkl", [128, NOC, E], BF16))
                    stg = zw.enter_context(nc.sbuf_tensor("stg", [128, 3, 512], F32))

                    def prep_w(wd, ncol):
                        for c in range(NOC):
                            sl = c % 2
                            P.ld(wkf[:, sl, 0:ncol], wd.ap()[c * 128:(c + 1) * 128, 0:ncol])
                            P.gld(wkh[:, c, 0:ncol], wkf[:, sl, 0:ncol])
                            P.tt("vector", wkl[:, c, 0:ncol], wkf[:, sl, 0:ncol],
                                 wkh[:, c, 0:ncol], ALU.subtract)

                    for mi, (wd, sp_d) in enumerate(((wq_d, qsp_d), (wk_d, ksp_d))):
                        prep_w(wd, E)
                        for oc in range(NOC):
                            for sc in range(NS5):
                                ss = slice(sc * SC, (sc + 1) * SC)
                                pairs = []
                                for i in range(NOC):
                                    wh = wkh[:, i, oc * 128:(oc + 1) * 128]
                                    wl = wkl[:, i, oc * 128:(oc + 1) * 128]
                                    pairs += [(wh, h2h[:, i, ss]), (wh, h2l[:, i, ss]), (wl, h2h[:, i, ss])]
                                pp = pA if sc % 2 == 0 else pB
                                P.mmg(pp[:, 0:SC], pairs)
                                ps = sc % 3
                                P.act(stg[:, ps, 0:SC], pp[:, 0:SC], ACTF.Silu)
                                ap4 = sp_d.ap().rearrange("c d (h t) -> h d c t", h=H)
                                for b_ in range(2):
                                    P.st(ap4[2 * oc + b_, :, sc * 4:(sc + 1) * 4, :],
                                         stg[b_ * 64:(b_ + 1) * 64, ps, 0:SC])
                    prep_w(wv_d, E)
                    for oc2 in range(E // 512):
                        cs = slice(oc2 * 512, (oc2 + 1) * 512)
                        for t in range(n_st):
                            ts_ = slice(t * 128, (t + 1) * 128)
                            pairs = []
                            for i in range(NOC):
                                pairs += [(h2h[:, i, ts_], wkh[:, i, cs]),
                                          (h2h[:, i, ts_], wkl[:, i, cs]),
                                          (h2l[:, i, ts_], wkh[:, i, cs])]
                            pp = pC if t % 2 == 0 else pD
                            P.mmg(pp[:, 0:512], pairs)
                            ps = t % 3
                            P.act(stg[:, ps, 0:512], pp[:, 0:512], ACTF.Silu)
                            P.st(vsp_d.ap()[ts_, oc2 * 512:(oc2 + 1) * 512], stg[:, ps, 0:512])
                    for i in range(NOC):
                        sl = i % 2
                        P.ld(wkf[:, sl, 0:Dh], wg1_d.ap()[i * 128:(i + 1) * 128, :])
                        P.gld(wkh[:, i, 0:Dh], wkf[:, sl, 0:Dh])
                        P.tt("vector", wkl[:, i, 0:Dh], wkf[:, sl, 0:Dh],
                             wkh[:, i, 0:Dh], ALU.subtract)
                    for sc in range(NS5):
                        ss = slice(sc * SC, (sc + 1) * SC)
                        pairs = []
                        for i in range(NOC):
                            wh = wkh[:, i, 0:Dh]
                            wl = wkl[:, i, 0:Dh]
                            pairs += [(wh, h2h[:, i, ss]), (wh, h2l[:, i, ss]), (wl, h2h[:, i, ss])]
                        pp = pA if sc % 2 == 0 else pB
                        P.mmg(pp[0:64, 0:SC], pairs)
                        P.cp("vector", g1f[:, ss], pp[0:64, 0:SC])
                    P.cp("vector", g1h[:], g1f[:])
                    P.tt("vector", g1l[:], g1f[:], g1h[:], ALU.subtract)
                    P.dma_fence()
                    P.phase("proj")

            zi = contextlib.ExitStack()
            with zi:
                lamkf_s = zi.enter_context(nc.sbuf_tensor("lamkf_s", [CH, H * Dh], F32))
                lamcf_s = zi.enter_context(nc.sbuf_tensor("lamcf_s", [64, H * Dh], F32))
                mask_s = zi.enter_context(nc.sbuf_tensor("mask_s", [CH, H, CH], F32))
                kf = zi.enter_context(nc.sbuf_tensor("kf", [64, 2, H * CH], F32))
                qf = zi.enter_context(nc.sbuf_tensor("qf", [64, 2, H * CH], F32))
                vt = zi.enter_context(nc.sbuf_tensor("vt", [128, 2, E], F32))
                vl = zi.enter_context(nc.sbuf_tensor("vl", [128, 2, E], F32))
                kt = zi.enter_context(nc.sbuf_tensor("kt", [128, 2, E], F32))
                smt = zi.enter_context(nc.sbuf_tensor("smt", [128, 2, H * CH], F32))
                toi = zi.enter_context(nc.sbuf_tensor("toi", [128, 2, E], F32))
                kvt = zi.enter_context(nc.sbuf_tensor("kvt", [64, 2, E], F32))
                och = zi.enter_context(nc.sbuf_tensor("och", [128, 2, E], F32))
                pW = zi.enter_context(nc.psum_tensor("pW", [128, 512], F32))
                P.warm_ap = pW[0:64, 0:128]
                P.ld(lamkf_s.ap(), lamkf_d.ap())
                P.ld(lamcf_s.ap(), lamcf_d.ap())
                P.ld(cv_s.ap(), cvec_d.ap())
                P.ld(mask_s.ap(), maskt_d.ap().rearrange("h p c -> p h c"))
                P.mset("vector", kv_a[:], 0.0)
                for c in range(n_st):
                    cs = slice(c * 128, (c + 1) * 128)
                    sl = c % 2
                    kv_cur = kv_a if c % 2 == 0 else kv_b
                    kv_nxt = kv_b if c % 2 == 0 else kv_a
                    P.ld(kf[:, sl, :], ksp_d.ap()[c])
                    P.ld(qf[:, sl, :], qsp_d.ap()[c])
                    P.ld(vt[:, sl, :], vsp_d.ap()[cs, :])
                    for h in range(H):
                        P.mm(banks[h // 4][:, (h % 4) * 128:(h % 4 + 1) * 128],
                             kf[:, sl, h * CH:(h + 1) * CH], qf[:, sl, h * CH:(h + 1) * CH])
                    P.warm()
                    for b4 in range(4):
                        P.tt("vector", smt[:, sl, b4 * 512:(b4 + 1) * 512], banks[b4][:, 0:512],
                             mask_s[:, 4 * b4:4 * (b4 + 1), :], ALU.mult)
                    for h in range(H):
                        pk = pE_ if h < 8 else pF
                        P.tr(pk[:, (h % 8) * 64:(h % 8 + 1) * 64],
                             kf[:, sl, h * CH:(h + 1) * CH], idf[0:64, 0:64])
                    P.cp("vector", kt[:, sl, 0:512], pE_[:, 0:512])
                    P.cp("vector", kt[:, sl, 512:1024], pF[:, 0:512])
                    P.warm()
                    P.tt("vector", vl[:, sl, :], vt[:, sl, :], lamkf_s[:], ALU.mult)
                    for h in range(H):
                        ds = slice(h * Dh, (h + 1) * Dh)
                        pi = banks[0] if h < 8 else banks[1]
                        P.mm(pi[:, (h % 8) * 64:(h % 8 + 1) * 64], smt[:, sl, h * CH:(h + 1) * CH],
                             vt[:, sl, ds])
                    for h in range(H):
                        ds = slice(h * Dh, (h + 1) * Dh)
                        po = banks[2] if h < 8 else banks[3]
                        P.mm(po[:, (h % 8) * 64:(h % 8 + 1) * 64], qf[:, sl, h * CH:(h + 1) * CH],
                             kv_cur[:, ds])
                    for h in range(H):
                        ds = slice(h * Dh, (h + 1) * Dh)
                        pa = pE_ if h < 8 else pF
                        P.mm(pa[0:64, (h % 8) * 64:(h % 8 + 1) * 64], kt[:, sl, ds], vl[:, sl, ds])
                    P.warm()
                    for h in range(H):
                        po = banks[2] if h < 8 else banks[3]
                        P.act(toi[:, sl, h * Dh:(h + 1) * Dh], po[:, (h % 8) * 64:(h % 8 + 1) * 64],
                              ACTF.Identity, scale=lamq_s[:, h:h + 1])
                    P.tt("vector", och[:, sl, 0:512], toi[:, sl, 0:512], banks[0][:, 0:512], ALU.add)
                    P.tt("vector", och[:, sl, 512:1024], toi[:, sl, 512:1024], banks[1][:, 0:512], ALU.add)
                    P.warm()
                    P.tt("vector", kvt[:, sl, :], kv_cur[:], lamcf_s[:], ALU.mult)
                    P.tt("vector", kv_nxt[:, 0:512], kvt[:, sl, 0:512], pE_[0:64, 0:512], ALU.add)
                    P.tt("vector", kv_nxt[:, 512:1024], kvt[:, sl, 512:1024], pF[0:64, 0:512], ALU.add)
                    P.warm()
                    P.st(osp_d.ap()[cs, :], och[:, sl, :])
                P.phase("scan")
                kv_fin = kv_a if n_st % 2 == 0 else kv_b
                P.st(cin_d.ap(), kv_fin.ap())
                groups = [list(range(g * cfg.group, (g + 1) * cfg.group))
                          for g in range(cfg.n_cores // cfg.group)]
                P.cc("AllGather", ALU.bypass, groups, cin_d.ap().opt(), cout_d.ap().opt())
                P.dma_fence_nocc()

            zb6 = contextlib.ExitStack()
            with zb6:
                qf6 = zb6.enter_context(nc.sbuf_tensor("qf6", [64, 2, H * CH], F32))
                och6 = zb6.enter_context(nc.sbuf_tensor("och6", [128, 2, E], F32))
                gmt = zb6.enter_context(nc.sbuf_tensor("gmt", [128, 2, E], F32))
                cor = zb6.enter_context(nc.sbuf_tensor("cor", [128, 1, E], F32))
                gat = zb6.enter_context(nc.sbuf_tensor("gat", [128, 1, E], F32))
                yv = zb6.enter_context(nc.sbuf_tensor("yv", [128, 1, E], F32))
                yh = zb6.enter_context(nc.sbuf_tensor("yh", [128, 1, E], BF16))
                yl = zb6.enter_context(nc.sbuf_tensor("yl", [128, 1, E], BF16))
                yth = zb6.enter_context(nc.sbuf_tensor("yth", [128, 1, NOC, 128], BF16))
                ytl = zb6.enter_context(nc.sbuf_tensor("ytl", [128, 1, NOC, 128], BF16))
                gab_s = zb6.enter_context(nc.sbuf_tensor("gab_s", [128, 2 * E], F32))
                woh = zb6.enter_context(nc.sbuf_tensor("woh", [128, NOC, E], BF16))
                wol = zb6.enter_context(nc.sbuf_tensor("wol", [128, NOC, E], BF16))
                wsf6 = zb6.enter_context(nc.sbuf_tensor("wsf6", [128, 2, E], F32))
                g2h = zb6.enter_context(nc.sbuf_tensor("g2h", [64, E], BF16))
                g2l = zb6.enter_context(nc.sbuf_tensor("g2l", [64, E], BF16))
                pTy = zb6.enter_context(nc.psum_tensor("pTy", [128, 1024], BF16))
                pW6 = zb6.enter_context(nc.psum_tensor("pW6", [128, 512], F32))
                P.warm_ap = pW6[0:64, 0:128]

                P.ld(gab_s.ap(), gab_d.ap())
                P.ld(wsf6[0:64, 0, :], wg2_d.ap())
                P.gld(g2h.ap(), wsf6[0:64, 0, :])
                P.tt("vector", g2l[:], wsf6[0:64, 0, :], g2h[:], ALU.subtract)
                for i in range(NOC):
                    wl_ = i % 2
                    P.ld(wsf6[:, wl_, :], wo_d.ap()[i * 128:(i + 1) * 128, :])
                    P.gld(woh[:, i, :], wsf6[:, wl_, :])
                    P.tt("vector", wol[:, i, :], wsf6[:, wl_, :], woh[:, i, :], ALU.subtract)
                P.mset("vector", kv_a[:], 0.0)
                for j in range(cfg.group):
                    P.gld(kv_b.ap(), cout_d.ap()[j * 64:(j + 1) * 64, :])
                    for h in range(H):
                        ds = slice(h * Dh, (h + 1) * Dh)
                        P.stt("vector", kv_a[:, ds], kv_b[:, ds],
                              cv_s[:, j * H + h:j * H + h + 1], kv_a[:, ds], ALU.mult, ALU.add)
                if dbg:
                    P.st(dkv_d.ap(), kv_a.ap())
                for c in range(n_st):
                    cs = slice(c * 128, (c + 1) * 128)
                    sl = c % 2
                    P.ld(qf6[:, sl, :], qsp_d.ap()[c])
                    P.ld(och6[:, sl, :], osp_d.ap()[cs, :])
                    P.ld(gmt[:, sl, :], gamf_d.ap()[cs, :])
                    P.warm()
                    for h in range(H):
                        ds = slice(h * Dh, (h + 1) * Dh)
                        pc_ = pA if h < 8 else pB
                        P.mm(pc_[:, (h % 8) * 64:(h % 8 + 1) * 64], qf6[:, sl, h * CH:(h + 1) * CH],
                             kv_a[:, ds])
                    P.tt("vector", cor[:, 0, 0:512], pA[:, 0:512], gmt[:, sl, 0:512], ALU.mult)
                    P.tt("vector", cor[:, 0, 512:1024], pB[:, 0:512], gmt[:, sl, 512:1024], ALU.mult)
                    P.tt("vector", och6[:, sl, :], och6[:, sl, :], cor[:, 0, :], ALU.add)
                    if dbg:
                        P.st(doc_d.ap()[cs, :], och6[:, sl, :])
                    P.warm()
                    P.act(yv[:, 0, :], och6[:, sl, :], ACTF.Identity, accum=st1[:, 0:1])
                    P.ts("vector", st1[:, 1:2], st1[:, 0:1], 1.0 / E, ALU.mult)
                    P.ts("vector", yv[:, 0, :], och6[:, sl, :], st1[:, 1:2], ALU.subtract)
                    P.act(gat[:, 0, 0:E], yv[:, 0, :], ACTF.Square, accum=st1[:, 2:3])
                    P.act(st1[:, 3:4], st1[:, 2:3], ACTF.Sqrt, bias=1e-5, scale=1.0 / E)
                    P.recip(st1[:, 4:5], st1[:, 3:4])
                    P.ts("vector", yv[:, 0, :], yv[:, 0, :], st1[:, 4:5], ALU.mult)
                    P.tt("vector", yv[:, 0, :], yv[:, 0, :], gab_s[:, 0:E], ALU.mult)
                    P.tt("vector", yv[:, 0, :], yv[:, 0, :], gab_s[:, E:2 * E], ALU.add)
                    P.warm()
                    for e2 in range(E // 512):
                        es = slice(e2 * 512, (e2 + 1) * 512)
                        pp = pC if e2 == 0 else pD
                        P.mmg(pp[:, 0:512], [(g1h[:, cs], g2h[:, es]),
                                             (g1h[:, cs], g2l[:, es]),
                                             (g1l[:, cs], g2h[:, es])])
                        P.act(gat[:, 0, es], pp[:, 0:512], ACTF.Sigmoid)
                    P.warm()
                    P.tt("vector", yv[:, 0, :], yv[:, 0, :], gat[:, 0, :], ALU.mult)
                    P.cp("vector", yh[:, 0, :], yv[:, 0, :])
                    P.tt("vector", yl[:, 0, :], yv[:, 0, :], yh[:, 0, :], ALU.subtract)
                    for i in range(NOC):
                        P.tr(pTy[:, i * 128:(i + 1) * 128], yh[:, 0, i * 128:(i + 1) * 128], idb[:])
                    P.cp("vector", yth[:, 0, :, :].rearrange("p c o -> p (c o)"), pTy[:, 0:1024])
                    for i in range(NOC):
                        P.tr(pTy[:, i * 128:(i + 1) * 128], yl[:, 0, i * 128:(i + 1) * 128], idb[:])
                    P.cp("vector", ytl[:, 0, :, :].rearrange("p c o -> p (c o)"), pTy[:, 0:1024])
                    for e2 in range(E // 512):
                        es = slice(e2 * 512, (e2 + 1) * 512)
                        pairs = []
                        for i in range(NOC):
                            pairs += [(yth[:, 0, i, :], woh[:, i, es]),
                                      (yth[:, 0, i, :], wol[:, i, es]),
                                      (ytl[:, 0, i, :], woh[:, i, es])]
                        pp = pE_ if e2 == 0 else pF
                        P.mmg(pp[:, 0:512], pairs)
                        P.tt("vector", xr[:, c, es], xr[:, c, es], pp[:, 0:512], ALU.add)
                P.phase("fin")
                P.dma_fence()
                P.phase("fin_end")

        P.warm_ap = pE_[0:64, 0:128]
        emit_ffn(first=False)

        for t in range(n_st):
            P.st(out_d.ap()[t * 128:(t + 1) * 128, :], xr[:, t, :], queue="fin")
        P.wait_all_dma("sync")
        P.wait_all_dma("gpsimd")
        P.phase("store")

    return nc


def core_inputs(cfg, core, full):
    tabs = host_tables(cfg)
    pos = core % cfg.group
    E = cfg.E
    xf = np.ascontiguousarray(full["x"].reshape(-1, E))
    sl = slice(core * cfg.Tc, (core + 1) * cfg.Tc)
    gab = np.concatenate([
        np.broadcast_to(full["gamma"].reshape(1, E), (128, E)),
        np.broadcast_to(full["beta"].reshape(1, E), (128, E))], axis=1)
    return dict(
        x0=np.ascontiguousarray(xf[sl]),
        wq=np.ascontiguousarray(full["Wq"]),
        wk=np.ascontiguousarray(full["Wk"]),
        wv=np.ascontiguousarray(full["Wv"]),
        wo=np.ascontiguousarray(full["Wo"]),
        wg1=np.ascontiguousarray(full["Wg1"]),
        wg2=np.ascontiguousarray(full["Wg2"]),
        wgt=np.ascontiguousarray(full["w_gate"].T),
        wut=np.ascontiguousarray(full["w_up"].T),
        wdt=np.ascontiguousarray(full["w_down"].T),
        gab=np.ascontiguousarray(gab),
        maskt=tabs["maskT"],
        lamq=tabs["lamq"],
        lamkf=tabs["lamk_full"],
        lamcf=tabs["lamc_full"],
        gamf=tabs["gam_full"],
        cvec=host_cvec(cfg, pos),
    )


from concourse.bass_utils import run_bass_kernel_spmd

_NC_CACHE = {}


def kernel(L0_attn_x, L0_attn_Wq, L0_attn_Wk, L0_attn_Wv, L0_attn_Wo,
           L0_attn_gamma, L0_attn_beta, L0_attn_Wg1, L0_attn_Wg2,
           L0_ffn_w_gate, L0_ffn_w_up, L0_ffn_w_down):
    cfg = CFG(Tc=2048, n_cores=8, group=4, sup=4)
    full = dict(
        x=np.asarray(L0_attn_x, np.float32),
        Wq=np.asarray(L0_attn_Wq, np.float32), Wk=np.asarray(L0_attn_Wk, np.float32),
        Wv=np.asarray(L0_attn_Wv, np.float32), Wo=np.asarray(L0_attn_Wo, np.float32),
        gamma=np.asarray(L0_attn_gamma, np.float32), beta=np.asarray(L0_attn_beta, np.float32),
        Wg1=np.asarray(L0_attn_Wg1, np.float32), Wg2=np.asarray(L0_attn_Wg2, np.float32),
        w_gate=np.asarray(L0_ffn_w_gate, np.float32), w_up=np.asarray(L0_ffn_w_up, np.float32),
        w_down=np.asarray(L0_ffn_w_down, np.float32),
    )
    if "nc" not in _NC_CACHE:
        _NC_CACHE["nc"] = build_nc(cfg)
    nc = _NC_CACHE["nc"]
    in_maps = [core_inputs(cfg, i, full) for i in range(cfg.n_cores)]
    res = run_bass_kernel_spmd(nc, in_maps, core_ids=list(range(cfg.n_cores)))
    parts = [np.asarray(res.results[i]["out"]) for i in range(cfg.n_cores)]
    out = np.concatenate(parts, axis=0).reshape(2, 8192, 1024).astype(np.float32)
    return out
